# version 01
# baseline (speedup 1.0000x reference)
import contextlib
import numpy as np
import ml_dtypes
import concourse.bass as bass
import concourse.mybir as mybir
from concourse.bass_utils import run_bass_kernel_spmd

F32 = mybir.dt.float32
BF16 = mybir.dt.bfloat16
AF = mybir.ActivationFunctionType
ALU = mybir.AluOpType
AX = mybir.AxisListType
BF = ml_dtypes.bfloat16

ENGINES = ["tensor", "vector", "scalar", "gpsimd", "sync"]

T = 4096
E = 1024
NT = 32
ND = 8
NH = 16
HD = 64


class Tok:
    __slots__ = ("eng", "val")
    def __init__(self, eng, val):
        self.eng = eng
        self.val = val


class Prog:
    def __init__(self):
        self.ops = {e: [] for e in ENGINES}
        self.count = {e: 0 for e in ENGINES}

    def emit(self, eng, fn, deps=(), inc=1):
        self.ops[eng].append((fn, [d for d in deps if d is not None], inc))
        self.count[eng] += inc
        return Tok(eng, self.count[eng])

    def dma(self, eng, out, in_, deps=()):
        return self.emit(eng, lambda e, out=out, in_=in_: e.dma_start(out=out, in_=in_),
                         deps, inc=16)

    def barrier(self):
        return [Tok(e, self.count[e]) for e in ENGINES if self.count[e] > 0]

    def build(self, nc, stack):
        sems = {e: stack.enter_context(nc.semaphore(f"sem_{e}")) for e in ENGINES}
        block = stack.enter_context(nc.Block())

        def make_section(eng):
            def body(engine):
                last = {e: 0 for e in ENGINES}
                for fn, deps, inc in self.ops[eng]:
                    for d in deps:
                        if d.eng == eng:
                            if d.val > last[eng]:
                                engine.wait_ge(sems[eng], d.val)
                                last[eng] = d.val
                        elif d.val > last[d.eng]:
                            engine.wait_ge(sems[d.eng], d.val)
                            last[d.eng] = d.val
                    fn(engine).then_inc(sems[eng], inc)
            return body

        for eng in ENGINES:
            getattr(block, eng)(make_section(eng))


def emit_ln(p, src_ap, out_ap, r1, r2, sqs, deps, W=1024, gamma=None, beta=None, gb_scr=None):
    t1 = p.emit("vector", lambda e: e.reduce_sum(r1, src_ap, axis=AX.X), deps)
    t2 = p.emit("vector", lambda e: e.tensor_scalar(r1, r1, 1.0 / W, None, ALU.mult), [t1])
    t3 = p.emit("vector", lambda e: e.tensor_scalar(out_ap, src_ap, r1, None, ALU.subtract), [t2] + list(deps))
    t4 = p.emit("scalar", lambda e: e.activation(sqs, out_ap, AF.Square), [t3])
    t5 = p.emit("vector", lambda e: e.reduce_sum(r2, sqs, axis=AX.X), [t4])
    t6 = p.emit("vector", lambda e: e.tensor_scalar(r2, r2, 1.0 / W, 1e-5, ALU.mult, ALU.add), [t5])
    t7 = p.emit("scalar", lambda e: e.activation(r2, r2, AF.Sqrt), [t6])
    t8 = p.emit("vector", lambda e: e.reciprocal(r2, r2), [t7])
    t9 = p.emit("vector", lambda e: e.tensor_scalar(out_ap, out_ap, r2, None, ALU.mult), [t8])
    if gamma is not None:
        t10 = p.emit("vector", lambda e: e.tensor_mul(out_ap, out_ap, gamma), [t9])
        t9 = p.emit("vector", lambda e: e.tensor_add(out_ap, out_ap, beta), [t10])
    return t9


def emit_xnt_tile(p, tt, xsrc_ap, xn, xnb, xnT_slices, r1, r2, sqs, psT, seyeb, deps, prev_evacs):
    lt = emit_ln(p, xsrc_ap, xn, r1, r2, sqs, deps)
    ct = p.emit("vector", lambda e: e.tensor_copy(xnb, xn), [lt])
    evacs = []
    tprev = prev_evacs
    for dc in range(ND):
        tr = p.emit("tensor",
                    lambda e, dc=dc: e.transpose(psT[:, 0:128], xnb[:, dc * 128:(dc + 1) * 128], seyeb),
                    [ct] + ([tprev[-1]] if tprev else []))
        ev = p.emit("vector",
                    lambda e, dc=dc: e.tensor_copy(xnT_slices(dc), psT[:, 0:128]), [tr])
        tprev = [ev]
        evacs.append(ev)
    return lt, ct, evacs


def build_block_a():
    nc = bass.Bass("TRN2", target_bir_lowering=False)
    x_in = nc.dram_tensor("x_in", [T, E], F32, kind="ExternalInput").ap()
    whead = nc.dram_tensor("whead", [NH, 128, ND, 192], F32, kind="ExternalInput").ap()
    wpT = nc.dram_tensor("wpT", [E, E], F32, kind="ExternalInput").ap()
    dmask = nc.dram_tensor("dmask", [4, 128, 512], BF16, kind="ExternalInput").ap()
    eyeb_d = nc.dram_tensor("eyeb", [128, 128], BF16, kind="ExternalInput").ap()
    xout = nc.dram_tensor("xout", [T, E], F32, kind="ExternalOutput").ap()

    p = Prog()
    stack = contextlib.ExitStack()
    sb = lambda name, shape, dt: stack.enter_context(nc.sbuf_tensor(name, shape, dt))
    _psn = [0]
    def ps(shape, dt):
        _psn[0] += 1
        return stack.enter_context(nc.psum_tensor(f"ps{_psn[0]}", shape, dt))

    xnT = sb("xnT", [128, ND, T], BF16)
    yT = sb("yT", [128, ND, T], BF16)
    seyeb = sb("seyeb", [128, 128], BF16)
    onesf = sb("onesf", [1, 64], F32)
    xbuf = sb("xbuf", [128, 2, E], F32)
    xn = sb("xn", [128, E], F32)
    xnb = sb("xnb", [128, E], BF16)
    r1 = sb("r1", [128, 1], F32)
    r2 = sb("r2", [128, 1], F32)
    sqs = sb("sqs", [128, E], F32)
    wpbuf = sb("wpbuf", [128, E], F32)
    wpTb = sb("wpTb", [128, ND, E], BF16)
    attn_sb = sb("attn_sb", [128, E], F32)
    outbuf = sb("outbuf", [128, E], F32)
    stack2 = contextlib.ExitStack()
    sb2 = lambda name, shape, dt: stack2.enter_context(nc.sbuf_tensor(name, shape, dt))
    masks = sb2("masks", [128, 4, 512], BF16)
    vo = sb2("vo", [128, NT, 65], BF16)
    vtb = xnb[0:64, 0:512]
    phiq = sb2("phiq", [64, T], BF16)
    phik = sb2("phik", [64, T], BF16)
    whbuf = sb2("whbuf", [128, 192], F32)
    whb = sb2("whb", [128, ND, 192], BF16)
    Wbuf = sb2("Wbuf", [128, 2, 512], BF16)
    dscr = xn[0:1, 512:1024]
    drcp = xbuf[0:1, 0, 0:512]
    numsb = xbuf[0:64, 1, 0:512]
    mn = sqs[:, 0:512]
    ex = sqs[:, 512:1024]
    rx = xn[:, 0:512]

    psT = ps([128, 128], BF16)
    psQK = ps([128, 512], F32)
    psA = ps([128, 512], F32)
    psO = ps([128, 512], F32)
    psB2 = ps([64, 512], F32)
    psP1 = ps([128, 512], F32)
    psP2 = ps([128, 512], F32)

    ld_eye = p.dma("sync", seyeb[:], eyeb_d)
    ld_mask = p.dma("sync", masks[:], dmask.rearrange("k p f -> p k f"))
    t_ones = p.emit("vector", lambda e: e.memset(onesf[:], 1.0))
    t_voones = p.emit("vector", lambda e: e.memset(vo[:, :, 64:65], 1.0))

    slot_free = [None, None]
    prev_ev = []
    for tt in range(NT):
        s = tt % 2
        ld = p.dma("sync", xbuf[:, s, :], x_in[tt * 128:(tt + 1) * 128, :],
                   [slot_free[s]])
        lt, ct, evs = emit_xnt_tile(
            p, tt, xbuf[:, s, :], xn[:], xnb[:],
            lambda dc, tt=tt: xnT[:, dc, tt * 128:(tt + 1) * 128],
            r1[:], r2[:], sqs[:], psT, seyeb[:], [ld, ld_eye], prev_ev)
        prev_ev = evs
        slot_free[s] = lt

    bar1 = p.barrier()

    w_free = None
    psqk_free = None
    psA_free = None
    psO_free = None
    psB2_free = None
    wslot_free = [None, None]
    vtb_free = None
    for h in range(NH):
        hp0 = (h % 2) * 64
        hdc = h // 2
        cw = None
        for dc in range(ND):
            ldw = p.dma("sync", whbuf[:], whead[h, :, dc, :], [w_free] + (bar1 if h == 0 else []))
            cw = p.emit("vector", lambda e, dc=dc: e.tensor_copy(whb[:, dc, :], whbuf[:]), [ldw, cw])
            w_free = cw
        phi_toks = []
        for tg in range(8):
            gs = slice(tg * 512, (tg + 1) * 512)
            mm = None
            for dc in range(ND):
                mm = p.emit("tensor",
                            lambda e, dc=dc, gs=gs: e.matmul(psQK[:], whb[:, dc, 0:128],
                                                             xnT[:, dc, gs],
                                                             start=(dc == 0), stop=(dc == ND - 1)),
                            ([cw, psqk_free] if dc == 0 else [mm]))
            last_rx = None
            for b0, dst in ((0, phiq), (64, phik)):
                mnh = sqs[b0:b0 + 64, 0:512]
                exh = sqs[b0:b0 + 64, 512:1024]
                rxh = xn[b0:b0 + 64, 0:512]
                t_mn = p.emit("vector", lambda e, b0=b0, mnh=mnh: e.tensor_scalar(mnh, psQK[b0:b0 + 64, :], 0.0, None, ALU.min), [mm])
                t_ex = p.emit("scalar", lambda e, mnh=mnh, exh=exh: e.activation(exh, mnh, AF.Exp), [t_mn])
                t_rx = p.emit("vector", lambda e, b0=b0, rxh=rxh: e.tensor_scalar(rxh, psQK[b0:b0 + 64, :], 0.0, None, ALU.max), [mm, t_mn])
                t_phi = p.emit("vector", lambda e, gs=gs, dst=dst, exh=exh, rxh=rxh: e.tensor_add(dst[:, gs], exh, rxh), [t_ex, t_rx])
                last_rx = t_rx
                phi_toks.append(t_phi)
            psqk_free = t_phi
        for tg in range(8):
            gs = slice(tg * 512, (tg + 1) * 512)
            mm = None
            for dc in range(ND):
                mm = p.emit("tensor",
                            lambda e, dc=dc, gs=gs: e.matmul(psQK[0:64, :], whb[:, dc, 128:192],
                                                             xnT[:, dc, gs],
                                                             start=(dc == 0), stop=(dc == ND - 1)),
                            ([psqk_free] if dc == 0 else [mm]))
            cv = p.emit("scalar", lambda e: e.copy(vtb, psQK[0:64, :]), [mm, vtb_free])
            psqk_free = cv
            trv = None
            for k in range(4):
                jc = tg * 4 + k
                trv = p.emit("tensor",
                             lambda e, k=k: e.transpose(psT[:, 0:64], vtb[:, k * 128:(k + 1) * 128], seyeb[0:64, 0:64]),
                             [cv] + ([trv] if trv else [prev_ev[-1] if prev_ev else None]))
                evv = p.emit("vector",
                             lambda e, jc=jc: e.tensor_copy(vo[:, jc, 0:64], psT[:, 0:64]),
                             [trv, t_voones])
                trv = evv
            vtb_free = trv
        for g in range(8):
            gs = slice(g * 512, (g + 1) * 512)
            njc = 4 * g + 4
            mmo_last = None
            for jc in range(njc):
                js = slice(jc * 128, (jc + 1) * 128)
                mmS = p.emit("tensor",
                             lambda e, js=js, gs=gs: e.matmul(psA[:], phik[:, js], phiq[:, gs],
                                                              start=True, stop=True, skip_group_check=True),
                             [psA_free] + (phi_toks if jc == 0 else []))
                wsl = jc % 2
                weng = "vector" if (jc % 2 == 0) else "scalar"
                if jc >= 4 * g:
                    mp = jc - 4 * g
                    wt = p.emit("vector",
                                lambda e, wsl=wsl, mp=mp: e.tensor_mul(Wbuf[:, wsl, :], psA[:], masks[:, mp, :]),
                                [mmS, wslot_free[wsl], ld_mask])
                else:
                    if weng == "vector":
                        wt = p.emit("vector", lambda e, wsl=wsl: e.tensor_copy(Wbuf[:, wsl, :], psA[:]),
                                    [mmS, wslot_free[wsl]])
                    else:
                        wt = p.emit("scalar", lambda e, wsl=wsl: e.copy(Wbuf[:, wsl, :], psA[:]),
                                    [mmS, wslot_free[wsl]])
                psA_free = wt
                mmo_last = p.emit("tensor",
                                  lambda e, jc=jc, wsl=wsl, njc=njc: e.matmul(
                                      psO[0:65, :], vo[:, jc, :], Wbuf[:, wsl, :],
                                      start=(jc == 0), stop=(jc == njc - 1), skip_group_check=True),
                                  [wt] + ([psO_free] if jc == 0 else []))
                wslot_free[wsl] = mmo_last
            td = p.emit("vector", lambda e: e.tensor_scalar(dscr, psO[64:65, :], 1e-6, None, ALU.add), [mmo_last])
            tr_ = p.emit("vector", lambda e: e.reciprocal(drcp, dscr), [td])
            tb2 = p.emit("tensor", lambda e: e.matmul(psB2[:], onesf[:], drcp, start=True, stop=True,
                                                      skip_group_check=True),
                         [tr_, t_ones, psB2_free])
            tnum = p.emit("scalar", lambda e: e.copy(numsb, psO[0:64, :]), [mmo_last])
            ty = p.emit("vector",
                        lambda e, hp0=hp0, hdc=hdc, gs=gs: e.tensor_mul(yT[hp0:hp0 + 64, hdc, gs],
                                                                        numsb, psB2[:]),
                        [tnum, tb2])
            psO_free = ty
            psB2_free = ty

    bar2 = p.barrier()
    stack2.close()

    cwp = None
    for dc in range(ND):
        ldp = p.dma("sync", wpbuf[:], wpT[dc * 128:(dc + 1) * 128, :], bar2 + [cwp])
        cwp = p.emit("vector", lambda e, dc=dc: e.tensor_copy(wpTb[:, dc, :], wpbuf[:]), [ldp])
    slot_free = [None, None]
    ost_free = [None, None]
    ev_free = None
    st_toks = []
    for tt in range(NT):
        s = tt % 2
        ts = slice(tt * 128, (tt + 1) * 128)
        mms = []
        for oh, psP in ((0, psP1), (1, psP2)):
            mm = None
            for dc in range(ND):
                mm = p.emit("tensor",
                            lambda e, dc=dc, oh=oh, psP=psP, ts=ts: e.matmul(
                                psP[:], yT[:, dc, ts], wpTb[:, dc, oh * 512:(oh + 1) * 512],
                                start=(dc == 0), stop=(dc == ND - 1)),
                            ([cwp, ev_free] if dc == 0 else [mm]))
            mms.append(mm)
        e1 = p.emit("scalar", lambda e: e.copy(attn_sb[:, 0:512], psP1[:]), [mms[0]])
        e2 = p.emit("scalar", lambda e: e.copy(attn_sb[:, 512:1024], psP2[:]), [mms[1]])
        ev_free = e2
        lt = emit_ln(p, attn_sb[:], sqs[:] , r1[:], r2[:], xn[:], [e1, e2])
        ldx = p.dma("sync", xbuf[:, s, :], x_in[ts, :], [slot_free[s]])
        ta = p.emit("vector", lambda e, s=s: e.tensor_add(outbuf[:], sqs[:], xbuf[:, s, :]), [lt, ldx, ost_free[0]])
        slot_free[s] = ta
        st = p.dma("gpsimd", xout[ts, :], outbuf[:], [ta])
        ost_free[0] = st
        st_toks.append(st)
    p.emit("gpsimd", lambda e: e.engine_nop(), st_toks)
    p.build(nc, stack)
    stack.close()
    return nc


def build_block_b():
    nc = bass.Bass("TRN2", target_bir_lowering=False)
    x_in = nc.dram_tensor("x_in", [T, E], F32, kind="ExternalInput").ap()
    sraw = nc.dram_tensor("sraw", [64, E], F32, kind="ExternalInput").ap()
    knots_d = nc.dram_tensor("knots", [128, 8], F32, kind="ExternalInput").ap()
    temp_d = nc.dram_tensor("temp", [128, 1], F32, kind="ExternalInput").ap()
    cw1_d = nc.dram_tensor("cw1", [E, 256], F32, kind="ExternalInput").ap()
    cb1_d = nc.dram_tensor("cb1", [128, 256], F32, kind="ExternalInput").ap()
    cw2_d = nc.dram_tensor("cw2", [2, 128, 2], F32, kind="ExternalInput").ap()
    cb2_d = nc.dram_tensor("cb2", [128, 2], F32, kind="ExternalInput").ap()
    spl_d = nc.dram_tensor("spl", [64, 65], F32, kind="ExternalInput").ap()
    dirs_d = nc.dram_tensor("dirs", [64, E], F32, kind="ExternalInput").ap()
    gam_d = nc.dram_tensor("gam", [128, E], F32, kind="ExternalInput").ap()
    bet_d = nc.dram_tensor("bet", [128, E], F32, kind="ExternalInput").ap()
    osc_d = nc.dram_tensor("osc", [128, 1], F32, kind="ExternalInput").ap()
    a8_d = nc.dram_tensor("a8", [64, 8], F32, kind="ExternalInput").ap()
    w8_d = nc.dram_tensor("w8", [128, 8], F32, kind="ExternalInput").ap()
    w64_d = nc.dram_tensor("w64", [128, 64], F32, kind="ExternalInput").ap()
    eyef_d = nc.dram_tensor("eyef", [128, 128], F32, kind="ExternalInput").ap()
    eyeb_d = nc.dram_tensor("eyeb", [128, 128], BF16, kind="ExternalInput").ap()
    xout = nc.dram_tensor("xout", [T, E], F32, kind="ExternalOutput").ap()

    p = Prog()
    stack = contextlib.ExitStack()
    sb = lambda name, shape, dt: stack.enter_context(nc.sbuf_tensor(name, shape, dt))
    _psn = [0]
    def ps(shape, dt):
        _psn[0] += 1
        return stack.enter_context(nc.psum_tensor(f"ps{_psn[0]}", shape, dt))

    sigall = sb("sigall", [128, E], F32)
    q1 = sb("q1", [64, E], F32)
    q2 = sb("q2", [64, E], F32)
    sa8 = sb("sa8", [8, 512], F32)
    sb8 = sb("sb8", [8, 512], F32)
    scat = sb("scat", [128, ND, 72], F32)
    a8s = sb("a8s", [64, 8], F32)
    knots = sb("knots_s", [128, 8], F32)
    dks = sb("dks", [128, 7], F32)
    invt = sb("invt", [128, 1], F32)
    cw1b = sb("cw1b", [128, ND, 256], BF16)
    cw1f = sb("cw1f", [128, ND, 256], F32)
    cb1t = sb("cb1t", [128, 256], F32)
    cw2b = sb("cw2b", [128, 2, 2], BF16)
    cw2f = sb("cw2f", [128, 2, 2], F32)
    cb2t = sb("cb2t", [128, 2], F32)
    spl = sb("spl_s", [64, 65], F32)
    dirs = sb("dirs_s", [64, E], F32)
    gam = sb("gam_s", [128, E], F32)
    bet = sb("bet_s", [128, E], F32)
    osc = sb("osc_s", [128, 1], F32)
    w8t = sb("w8t", [128, 8], F32)
    w64t = sb("w64t", [128, 64], F32)
    eyef = sb("eyef_s", [128, 128], F32)
    eyeb = sb("eyeb_s", [128, 128], BF16)
    xbuf = sb("xbuf", [128, 2, E], F32)
    xn = sb("xn", [128, E], F32)
    r1 = sb("r1", [128, 1], F32)
    r2 = sb("r2", [128, 1], F32)
    sqs = sb("sqs", [128, E], F32)
    xnTf = sb("xnTf", [128, ND, 128], F32)
    xnTb = sb("xnTb", [128, ND, 128], BF16)
    fsb = sb("fsb", [128, 72], F32)
    gsb = sb("gsb", [128, 72], F32)
    csb = sb("csb", [128, 72], F32)
    e8 = sb("e8", [128, 8], F32)
    oh8 = sb("oh8", [128, 8], F32)
    oh64 = sb("oh64", [128, 64], F32)
    tm64 = sb("tm64", [128, 64], F32)
    msb = sb("msb", [128, 64], F32)
    e64 = sb("e64", [128, 64], F32)
    ohsb = sb("ohsb", [128, 64], F32)
    ohT = sb("ohT", [64, 128], F32)
    hgel = sb("hgel", [128, 256], F32)
    hgb = sb("hgb", [128, 256], BF16)
    hTb = sb("hTb", [128, 2, 128], BF16)
    comp = sb("comp", [128, 2], F32)
    Wsel = sb("Wsel", [128, 65], F32)
    tsp = sb("tsp", [128, 16], F32)
    hhs = sb("hhs", [128, 16], F32)
    magv = sb("magv", [128, 1], F32)
    ysb = sb("ysb", [128, E], F32)
    zsb = sb("zsb", [128, E], F32)
    outbuf = sb("outbuf", [128, 2, E], F32)

    psTf = ps([128, 128], F32)
    psTb = ps([128, 128], BF16)
    psS = ps([128, 128], F32)
    psH = ps([128, 256], F32)
    psC2 = ps([128, 16], F32)
    psW = ps([128, 128], F32)
    psD1 = ps([128, 512], F32)
    psD2 = ps([128, 512], F32)

    l_eye = p.dma("sync", eyef[:], eyef_d)
    l_eyb = p.dma("sync", eyeb[:], eyeb_d)
    l_sr = p.dma("sync", sigall[0:64, :], sraw)
    l_a8 = p.dma("sync", a8s[:], a8_d)
    l_kn = p.dma("sync", knots[:], knots_d)
    l_tp = p.dma("sync", invt[:], temp_d)
    l_cw1 = p.dma("sync", cw1f[:], cw1_d.rearrange("(a p) b -> p a b", p=128))
    l_cb1 = p.dma("sync", cb1t[:], cb1_d)
    l_cw2 = p.dma("sync", cw2f[:], cw2_d.rearrange("a p b -> p a b"))
    l_cb2 = p.dma("sync", cb2t[:], cb2_d)
    l_spl = p.dma("sync", spl[:], spl_d)
    l_dir = p.dma("sync", dirs[:], dirs_d)
    l_gam = p.dma("sync", gam[:], gam_d)
    l_bet = p.dma("sync", bet[:], bet_d)
    l_osc = p.dma("sync", osc[:], osc_d)
    l_w8 = p.dma("sync", w8t[:], w8_d)
    l_w64 = p.dma("sync", w64t[:], w64_d)

    tq1 = p.emit("vector", lambda e: e.tensor_scalar(q1[:], sigall[0:64, :], 0.3, None, ALU.is_gt), [l_sr])
    tq2 = p.emit("vector", lambda e: e.tensor_scalar(q2[:], sigall[0:64, :], -0.3, None, ALU.is_lt), [l_sr])
    tq = p.emit("vector", lambda e: e.tensor_sub(sigall[0:64, :], q1[:], q2[:]), [tq1, tq2])
    tcs = []
    for half in range(2):
        hs = slice(half * 512, (half + 1) * 512)
        m1 = p.emit("tensor", lambda e, hs=hs: e.matmul(psD1[0:8, :], a8s[:], sigall[0:64, hs],
                                                        start=True, stop=True), [tq, l_a8] + tcs)
        s1 = p.emit("vector", lambda e: e.tensor_scalar(sa8[:], psD1[0:8, :], 0.0, None, ALU.is_gt), [m1])
        s2 = p.emit("vector", lambda e: e.tensor_scalar(sb8[:], psD1[0:8, :], 0.0, None, ALU.is_lt), [m1])
        s3 = p.emit("vector", lambda e, hs=hs: e.tensor_sub(sigall[64:72, hs], sa8[:], sb8[:]), [s1, s2])
        tcs = [s3]
    trp = None
    for dc in range(ND):
        tr = p.emit("tensor", lambda e, dc=dc: e.transpose(psTf[:], sigall[:, dc * 128:(dc + 1) * 128], eyef[:]),
                    [tcs[0], l_eye] + ([trp] if trp else []))
        ev = p.emit("vector", lambda e, dc=dc: e.tensor_copy(scat[:, dc, 0:72], psTf[:, 0:72]), [tr])
        trp = ev
    tdk = p.emit("vector", lambda e: e.tensor_sub(dks[:], knots[:, 1:8], knots[:, 0:7]), [l_kn])
    ti1 = p.emit("vector", lambda e: e.tensor_scalar(invt[:], invt[:], 1e-6, None, ALU.add), [l_tp])
    ti2 = p.emit("vector", lambda e: e.reciprocal(invt[:], invt[:]), [ti1])
    tc1 = p.emit("vector", lambda e: e.tensor_copy(cw1b[:].rearrange("p a b -> p (a b)"),
                                                   cw1f[:].rearrange("p a b -> p (a b)")), [l_cw1])
    tc2 = p.emit("vector", lambda e: e.tensor_copy(cw2b[:].rearrange("p a b -> p (a b)"),
                                                   cw2f[:].rearrange("p a b -> p (a b)")), [l_cw2])
    bar0 = p.barrier()

    slot_free = [None, None]
    ost_free = [None, None]
    st_toks = []
    prev = {}
    for tt in range(NT):
        s = tt % 2
        ts = slice(tt * 128, (tt + 1) * 128)
        ld = p.dma("sync", xbuf[:, s, :], x_in[ts, :], [slot_free[s]] + (bar0 if tt == 0 else []))
        lt = emit_ln(p, xbuf[:, s, :], xn[:], r1[:], r2[:], sqs[:], [ld] + [prev.get("xn_used")])
        trp = prev.get("trp")
        ev_toks = []
        for dc in range(ND):
            tr = p.emit("tensor", lambda e, dc=dc: e.transpose(psTf[:], xn[:, dc * 128:(dc + 1) * 128], eyef[:]),
                        [lt] + ([trp] if trp else []))
            ev = p.emit("vector", lambda e, dc=dc: e.tensor_copy(xnTf[:, dc, :], psTf[:]), [tr, prev.get("mmS_done"), prev.get("mmH_done")])
            cb = p.emit("scalar", lambda e, dc=dc: e.copy(xnTb[:, dc, :], xnTf[:, dc, :]), [ev])
            trp = ev
            ev_toks.append((ev, cb))
        prev["trp"] = trp
        mm = None
        for dc in range(ND):
            mm = p.emit("tensor", lambda e, dc=dc: e.matmul(psS[:, 0:72], xnTf[:, dc, :], scat[:, dc, :],
                                                            start=(dc == 0), stop=(dc == ND - 1)),
                        ([ev_toks[dc][0]] if dc else [ev_toks[0][0], prev.get("psS_used")]))
            if dc < ND - 1:
                mm = p.emit("tensor", lambda e: e.engine_nop(), [mm]) if False else mm
                pass
        tf1 = p.emit("vector", lambda e: e.tensor_scalar(fsb[:], psS[:, 0:72], invt[:], None, ALU.mult), [mm, ti2])
        prev["psS_used"] = tf1
        tf2 = p.emit("scalar", lambda e: e.activation(fsb[:], fsb[:], AF.Sigmoid), [tf1])
        tf3 = p.emit("vector", lambda e: e.tensor_scalar(fsb[:], fsb[:], 7.0, None, ALU.mult), [tf2])
        tg_ = p.emit("vector", lambda e: e.tensor_scalar(gsb[:], fsb[:], 0.0, knots[:, 0:1], ALU.mult, ALU.add), [tf3, l_kn])
        for j in range(7):
            ta = p.emit("vector", lambda e, j=j: e.tensor_scalar(csb[:], fsb[:], float(j), 0.0, ALU.subtract, ALU.max), [tf3, tg_])
            tb = p.emit("vector", lambda e: e.tensor_scalar(csb[:], csb[:], 1.0, None, ALU.min), [ta])
            tg_ = p.emit("vector", lambda e, j=j: e.scalar_tensor_tensor(gsb[:], csb[:], dks[:, j:j + 1], gsb[:],
                                                                         ALU.mult, ALU.add), [tb, tdk, tg_])
        tm = p.emit("vector", lambda e: e.reduce_max(r1[:], gsb[:, 64:72], axis=AX.X), [tg_])
        te8 = p.emit("vector", lambda e: e.tensor_scalar(e8[:], gsb[:, 64:72], r1[:], None, ALU.is_ge), [tm])
        tv8 = p.emit("vector", lambda e: e.tensor_mul(e8[:], e8[:], w8t[:]), [te8, l_w8])
        tr8 = p.emit("vector", lambda e: e.reduce_max(r1[:], e8[:], axis=AX.X), [tv8])
        to8 = p.emit("vector", lambda e: e.tensor_scalar(oh8[:], e8[:], r1[:], None, ALU.is_equal), [tr8])
        tr_l = []
        for l in range(8):
            tr_l.append(p.emit("vector", lambda e, l=l: e.tensor_copy(oh64[:, l:64:8].rearrange("p (a b) -> p a b", b=1) if False else oh64[:, l:64:8], oh8[:]), [to8]))
        tmm = p.emit("vector", lambda e: e.tensor_scalar(tm64[:], oh64[:], 1.0, 1e30, ALU.subtract, ALU.mult), tr_l)
        tms = p.emit("vector", lambda e: e.tensor_add(msb[:], tm64[:], gsb[:, 0:64]), [tmm])
        tm2 = p.emit("vector", lambda e: e.reduce_max(r1[:], msb[:], axis=AX.X), [tms])
        te64 = p.emit("vector", lambda e: e.tensor_scalar(e64[:], msb[:], r1[:], None, ALU.is_ge), [tm2])
        tv64 = p.emit("vector", lambda e: e.tensor_mul(e64[:], e64[:], w64t[:]), [te64, l_w64])
        tr64 = p.emit("vector", lambda e: e.reduce_max(r1[:], e64[:], axis=AX.X), [tv64])
        toh = p.emit("vector", lambda e: e.tensor_scalar(ohsb[:], e64[:], r1[:], None, ALU.is_equal), [tr64])
        t_tr = p.emit("tensor", lambda e: e.transpose(psTf[0:64, 0:128], ohsb[:], eyef[:]),
                      [toh, prev.get("trp")])
        prev["trp"] = t_tr
        t_ohT = p.emit("vector", lambda e: e.tensor_copy(ohT[:], psTf[0:64, :]), [t_tr])
        mmh = None
        for dc in range(ND):
            mmh = p.emit("tensor", lambda e, dc=dc: e.matmul(psH[:], xnTb[:, dc, :], cw1b[:, dc, :],
                                                             start=(dc == 0), stop=(dc == ND - 1)),
                         ([ev_toks[dc][1]] if dc else [ev_toks[0][1], tc1, prev.get("psH_used")]))
        th1 = p.emit("vector", lambda e: e.tensor_add(hgel[:], psH[:], cb1t[:]), [mmh, l_cb1])
        prev["psH_used"] = th1
        prev["mmH_done"] = mmh
        th2 = p.emit("scalar", lambda e: e.activation(hgel[:], hgel[:], AF.Gelu), [th1])
        th3 = p.emit("vector", lambda e: e.tensor_copy(hgb[:], hgel[:]), [th2])
        trh = None
        for kc in range(2):
            trh = p.emit("tensor", lambda e, kc=kc: e.transpose(psTb[:], hgb[:, kc * 128:(kc + 1) * 128], eyeb[:]),
                         [th3, l_eyb] + ([trh] if trh else [prev.get("psTb_used")]))
            evh = p.emit("vector", lambda e, kc=kc: e.tensor_copy(hTb[:, kc, :], psTb[:]), [trh])
            trh = evh
        prev["psTb_used"] = trh
        mmc = None
        for kc in range(2):
            mmc = p.emit("tensor", lambda e, kc=kc: e.matmul(psC2[:, 0:2], hTb[:, kc, :], cw2b[:, kc, :],
                                                             start=(kc == 0), stop=(kc == 1)),
                         [trh, tc2] if kc == 0 else [mmc])
        tcp1 = p.emit("vector", lambda e: e.tensor_add(comp[:], psC2[:, 0:2], cb2t[:]), [mmc, l_cb2])
        tcp = p.emit("scalar", lambda e: e.activation(comp[:], comp[:], AF.Tanh), [tcp1])
        mw = p.emit("tensor", lambda e: e.matmul(psW[:, 0:65], ohT[:], spl[:], start=True, stop=True),
                    [t_ohT, l_spl, prev.get("psW_used")])
        tws = p.emit("scalar", lambda e: e.copy(Wsel[:], psW[:, 0:65]), [mw])
        prev["psW_used"] = tws
        md1 = p.emit("tensor", lambda e: e.matmul(psD1[:], ohT[:], dirs[:, 0:512], start=True, stop=True),
                     [t_ohT, l_dir, prev.get("psD1_used")])
        md2 = p.emit("tensor", lambda e: e.matmul(psD2[:], ohT[:], dirs[:, 512:1024], start=True, stop=True),
                     [t_ohT, l_dir, prev.get("psD2_used")])
        ts1 = p.emit("vector", lambda e: e.scalar_tensor_tensor(tsp[:], Wsel[:, 0:16], comp[:, 0:1], Wsel[:, 32:48],
                                                                ALU.mult, ALU.add), [tws, tcp])
        ts2 = p.emit("vector", lambda e: e.scalar_tensor_tensor(tsp[:], Wsel[:, 16:32], comp[:, 1:2], tsp[:],
                                                                ALU.mult, ALU.add), [ts1])
        ts3 = p.emit("vector", lambda e: e.tensor_scalar(hhs[:], tsp[:], 0.0, None, ALU.max), [ts2])
        ts4 = p.emit("vector", lambda e: e.tensor_mul(hhs[:], hhs[:], Wsel[:, 48:64]), [ts3])
        ts5 = p.emit("vector", lambda e: e.reduce_sum(magv[:], hhs[:], axis=AX.X), [ts4])
        ts6 = p.emit("vector", lambda e: e.tensor_add(magv[:], magv[:], Wsel[:, 64:65]), [ts5])
        ts7 = p.emit("vector", lambda e: e.tensor_mul(magv[:], magv[:], osc[:]), [ts6, l_osc])
        ty1 = p.emit("vector", lambda e: e.scalar_tensor_tensor(ysb[:, 0:512], psD1[:], magv[:], xn[:, 0:512],
                                                                ALU.mult, ALU.add), [md1, ts7, lt])
        ty2 = p.emit("vector", lambda e: e.scalar_tensor_tensor(ysb[:, 512:1024], psD2[:], magv[:], xn[:, 512:1024],
                                                                ALU.mult, ALU.add), [md2, ts7, lt])
        prev["psD1_used"] = ty1
        prev["psD2_used"] = ty2
        prev["xn_used"] = ty2
        lz = emit_ln(p, ysb[:], zsb[:], r1[:], r2[:], sqs[:], [ty1, ty2], gamma=gam[:], beta=bet[:])
        lp = emit_ln(p, zsb[:], ysb[:], r1[:], r2[:], sqs[:], [lz])
        ta = p.emit("vector", lambda e, s=s: e.tensor_add(outbuf[:, s, :], ysb[:], xbuf[:, s, :]),
                    [lp, ost_free[s]])
        slot_free[s] = ta
        st = p.dma("gpsimd", xout[ts, :], outbuf[:, s, :], [ta])
        ost_free[s] = st
        st_toks.append(st)
        prev["mmS_done"] = mm
    p.emit("gpsimd", lambda e: e.engine_nop(), st_toks)
    p.build(nc, stack)
    stack.close()
    return nc


def build_block_c():
    nc = bass.Bass("TRN2", target_bir_lowering=False)
    x_in = nc.dram_tensor("x_in", [T, E], F32, kind="ExternalInput").ap()
    whd = nc.dram_tensor("whd", [NH, 128, ND, 192], F32, kind="ExternalInput").ap()
    bqk_d = nc.dram_tensor("bqk", [128, NH], F32, kind="ExternalInput").ap()
    bv_d = nc.dram_tensor("bv", [64, NH], F32, kind="ExternalInput").ap()
    wout_d = nc.dram_tensor("wout", [E, E], F32, kind="ExternalInput").ap()
    bout_d = nc.dram_tensor("bout", [128, E], F32, kind="ExternalInput").ap()
    cos_d = nc.dram_tensor("cosT", [64, T], F32, kind="ExternalInput").ap()
    sin_d = nc.dram_tensor("sinT", [64, T], F32, kind="ExternalInput").ap()
    perm_d = nc.dram_tensor("perm", [64, 64], F32, kind="ExternalInput").ap()
    eyeb_d = nc.dram_tensor("eyeb", [128, 128], BF16, kind="ExternalInput").ap()
    xout = nc.dram_tensor("xout", [T, E], F32, kind="ExternalOutput").ap()

    p = Prog()
    stack = contextlib.ExitStack()
    sb = lambda name, shape, dt: stack.enter_context(nc.sbuf_tensor(name, shape, dt))
    _psn = [0]
    def ps(shape, dt):
        _psn[0] += 1
        return stack.enter_context(nc.psum_tensor(f"ps{_psn[0]}", shape, dt))

    yTd = nc.dram_tensor("yTd", [ND, 128, T], BF16, kind="Internal").ap()
    xnT = sb("xnT", [128, ND, T], BF16)
    seyeb = sb("seyeb", [128, 128], BF16)
    cosT = sb("cosT_s", [64, T], F32)
    sinT = sb("sinT_s", [64, T], F32)
    permf = sb("permf", [64, 64], F32)
    bqk = sb("bqk_s", [128, NH], F32)
    bv = sb("bv_s", [64, NH], F32)
    xbuf = sb("xbuf", [128, 2, E], F32)
    xn = sb("xn", [128, E], F32)
    xnb = sb("xnb", [128, E], BF16)
    r1 = sb("r1", [128, 1], F32)
    r2 = sb("r2", [128, 1], F32)
    sqs = sb("sqs", [128, E], F32)
    wpbuf = sb("wpbuf", [128, E], F32)
    wpTb = sb("wpTb", [128, ND, E], BF16)
    attn_sb = sb("attn_sb", [128, E], F32)
    bout_t = sb("bout_t", [128, E], F32)
    outbuf = sb("outbuf", [128, 2, E], F32)
    ytt = sb("ytt", [128, ND, 128], BF16)
    stack2 = contextlib.ExitStack()
    sb2 = lambda name, shape, dt: stack2.enter_context(nc.sbuf_tensor(name, shape, dt))
    vo = sb2("vo", [128, NT, 64], BF16)
    vtb = sb2("vtb", [64, 512], BF16)
    phbq = sb2("phbq", [64, T], BF16)
    phbk = sb2("phbk", [64, T], BF16)
    rotq = sb2("rotq", [64, T], BF16)
    rotk = sb2("rotk", [64, T], BF16)
    qkb = xn[:, 0:512]
    phtq = xbuf[0:64, 0, 0:512]
    phtk = xbuf[0:64, 0, 512:1024]
    mn = sqs[:, 0:512]
    ex = sqs[:, 512:1024]
    tmpr = xbuf[0:64, 1, 0:512]
    rotb = xbuf[0:64, 1, 512:1024]
    whbuf = sb2("whbuf", [128, ND, 192], F32)
    whb = sb2("whb", [128, ND, 192], BF16)
    Wbuf = sb2("Wbuf", [128, 2, 512], BF16)
    numf = sb2("numf", [128, 512], F32)
    rdsb = sb2("rdsb", [128, 512], F32)
    ysb = sb2("ysb", [64, 512], BF16)

    psT = ps([128, 128], BF16)
    psQK = ps([128, 512], F32)
    psSw = ps([64, 512], F32)
    psN = ps([128, 512], F32)
    psD2 = ps([128, 512], F32)
    psO = ps([64, 512], F32)
    psP1 = ps([128, 512], F32)
    psP2 = ps([128, 512], F32)

    l_eye = p.dma("sync", seyeb[:], eyeb_d)
    l_cos = p.dma("sync", cosT[:], cos_d)
    l_sin = p.dma("sync", sinT[:], sin_d)
    l_perm = p.dma("sync", permf[:], perm_d)
    l_bqk = p.dma("sync", bqk[:], bqk_d)
    l_bv = p.dma("sync", bv[:], bv_d)

    slot_free = [None, None]
    prev_ev = []
    for tt in range(NT):
        s = tt % 2
        ld = p.dma("sync", xbuf[:, s, :], x_in[tt * 128:(tt + 1) * 128, :], [slot_free[s]])
        lt, ct, evs = emit_xnt_tile(
            p, tt, xbuf[:, s, :], xn[:], xnb[:],
            lambda dc, tt=tt: xnT[:, dc, tt * 128:(tt + 1) * 128],
            r1[:], r2[:], sqs[:], psT, seyeb[:], [ld, l_eye], prev_ev)
        prev_ev = evs
        slot_free[s] = lt

    bar1 = p.barrier()

    w_free = None
    psqk_free = None
    pssw_free = None
    vtb_free = None
    psN_free = [None, None]
    pr_ysb_free = [None]
    prev_w = None
    psD_free = [None, None]
    psO_free = None
    wslot_free = [None, None]
    nslot_free = [None, None]
    rslot_free = [None, None]
    for h in range(NH):
        hp0 = (h % 2) * 64
        hdc = h // 2
        ldw = p.dma("sync", whbuf[:], whd[h], [w_free] + (bar1 if h == 0 else []))
        cw = p.emit("vector", lambda e: e.tensor_copy(whb[:].rearrange("p a b -> p (a b)"),
                                                      whbuf[:].rearrange("p a b -> p (a b)")), [ldw])
        w_free = cw
        phi_toks = []
        for tg in range(8):
            gs = slice(tg * 512, (tg + 1) * 512)
            mm = None
            for dc in range(ND):
                mm = p.emit("tensor",
                            lambda e, dc=dc, gs=gs: e.matmul(psQK[:], whb[:, dc, 0:128], xnT[:, dc, gs],
                                                             start=(dc == 0), stop=(dc == ND - 1)),
                            ([cw, psqk_free] if dc == 0 else [mm]))
            tb_ = p.emit("vector", lambda e, h=h: e.tensor_scalar(qkb, psQK[:], bqk[:, h:h + 1], None, ALU.add),
                         [mm, l_bqk])
            psqk_free = tb_
            for b0, pht, phdst, rotdst in ((0, phtq, phbq, rotq), (64, phtk, phbk, rotk)):
                if b0 == 0:
                    mnh = sqs[0:64, 0:512]
                    exh = sqs[0:64, 512:1024]
                else:
                    mnh = attn_sb[0:64, 0:512]
                    exh = attn_sb[0:64, 512:1024]
                t_mn = p.emit("vector", lambda e, b0=b0, mnh=mnh: e.tensor_scalar(mnh, qkb[b0:b0 + 64, :], 0.0, None, ALU.min), [tb_])
                t_ex = p.emit("scalar", lambda e, mnh=mnh, exh=exh: e.activation(exh, mnh, AF.Exp), [t_mn])
                t_rx = p.emit("gpsimd", lambda e, b0=b0, pht=pht: e.tensor_scalar(pht, qkb[b0:b0 + 64, :], 0.0, None, ALU.max), [tb_])
                t_phi = p.emit("vector", lambda e, pht=pht, exh=exh: e.tensor_add(pht, exh, pht), [t_ex, t_rx])
                t_pb = p.emit("scalar", lambda e, gs=gs, phdst=phdst, pht=pht: e.copy(phdst[:, gs], pht), [t_phi])
                msw = p.emit("tensor",
                             lambda e, pht=pht: e.matmul(psSw[:], permf[:], pht,
                                                         start=True, stop=True, skip_group_check=True),
                             [t_phi, l_perm, pssw_free])
                t_t = p.emit("vector", lambda e, gs=gs: e.tensor_mul(tmpr, psSw[:], sinT[:, gs]), [msw, l_sin])
                pssw_free = t_t
                t_r = p.emit("gpsimd", lambda e, pht=pht, gs=gs: e.tensor_mul(rotb, pht, cosT[:, gs]), [t_phi, l_cos, t_t])
                t_a = p.emit("vector", lambda e, gs=gs, rotdst=rotdst: e.tensor_add(rotdst[:, gs], rotb, tmpr), [t_t, t_r])
                pssw_free = t_a
                phi_toks.append(Tok(t_pb.eng, t_pb.val))
                phi_toks.append(t_a)
        for tg in range(8):
            gs = slice(tg * 512, (tg + 1) * 512)
            mm = None
            for dc in range(ND):
                mm = p.emit("tensor",
                            lambda e, dc=dc, gs=gs: e.matmul(psQK[0:64, :], whb[:, dc, 128:192], xnT[:, dc, gs],
                                                             start=(dc == 0), stop=(dc == ND - 1)),
                            ([psqk_free] if dc == 0 else [mm]))
            cv = p.emit("vector", lambda e, h=h: e.tensor_scalar(vtb[:], psQK[0:64, :], bv[:, h:h + 1], None, ALU.add),
                        [mm, l_bv, vtb_free])
            psqk_free = cv
            trv = None
            for k in range(4):
                jc = tg * 4 + k
                trv = p.emit("tensor",
                             lambda e, k=k: e.transpose(psT[:, 0:64], vtb[0:64, k * 128:(k + 1) * 128], seyeb[0:64, 0:64]),
                             [cv] + ([trv] if trv else [prev_ev[-1] if prev_ev else None]))
                evv = p.emit("vector", lambda e, jc=jc: e.tensor_copy(vo[:, jc, :], psT[:, 0:64]), [trv])
                trv = evv
            vtb_free = trv
        for g in range(8):
            gs = slice(g * 512, (g + 1) * 512)
            mmo_last = None
            for jc in range(NT):
                js = slice(jc * 128, (jc + 1) * 128)
                sl = jc % 2
                mmN = p.emit("tensor",
                             lambda e, js=js, gs=gs: e.matmul(psN[:], rotk[:, js], rotq[:, gs],
                                                              start=True, stop=True, skip_group_check=True),
                             [psN_free[sl]] + (phi_toks if jc == 0 and g == 0 else []))
                mmD = p.emit("tensor",
                             lambda e, js=js, gs=gs: e.matmul(psD2[:], phbk[:, js], phbq[:, gs],
                                                              start=True, stop=True, skip_group_check=True),
                             [psD_free[sl]])
                t_rd = p.emit("vector", lambda e: e.reciprocal(rdsb[:], psD2[:]), [mmD, prev_w])
                psD_free[sl] = t_rd
                t_nv = p.emit("scalar", lambda e: e.copy(numf[:], psN[:]), [mmN, prev_w])
                psN_free[sl] = t_nv
                t_w = p.emit("gpsimd", lambda e, sl=sl: e.tensor_mul(Wbuf[:, sl, :], numf[:], rdsb[:]),
                             [t_rd, t_nv, wslot_free[sl]])
                prev_w = t_w
                mmo_last = p.emit("tensor",
                                  lambda e, jc=jc, sl=sl: e.matmul(psO[:], vo[:, jc, :], Wbuf[:, sl, :],
                                                                   start=(jc == 0), stop=(jc == NT - 1),
                                                                   skip_group_check=True),
                                  [t_w] + ([psO_free] if jc == 0 else []))
                wslot_free[sl] = mmo_last
            ty = p.emit("scalar",
                        lambda e: e.copy(ysb[:], psO[:]),
                        [mmo_last, pr_ysb_free[0]])
            psO_free = ty
            tyd = p.dma("gpsimd", yTd[hdc, hp0:hp0 + 64, gs], ysb[:], [ty])
            pr_ysb_free[0] = tyd

    bar2 = p.barrier()
    stack2.close()

    l_bo = p.dma("sync", bout_t[:], bout_d, bar2)
    cwp = None
    for dc in range(ND):
        ldp = p.dma("sync", wpbuf[:], wout_d[dc * 128:(dc + 1) * 128, :], bar2 + [cwp])
        cwp = p.emit("vector", lambda e, dc=dc: e.tensor_copy(wpTb[:, dc, :], wpbuf[:]), [ldp])
    slot_free = [None, None]
    ost_free = [None, None]
    ev_free = None
    st_toks = []
    yld_free = None
    for tt in range(NT):
        s = tt % 2
        ts = slice(tt * 128, (tt + 1) * 128)
        yld = [p.dma("sync", ytt[:, dc, :], yTd[dc, :, ts], [yld_free]) for dc in range(ND)]
        mms = []
        for oh, psP in ((0, psP1), (1, psP2)):
            mm = None
            for dc in range(ND):
                mm = p.emit("tensor",
                            lambda e, dc=dc, oh=oh, psP=psP: e.matmul(
                                psP[:], ytt[:, dc, :], wpTb[:, dc, oh * 512:(oh + 1) * 512],
                                start=(dc == 0), stop=(dc == ND - 1)),
                            ([cwp, ev_free] + yld if dc == 0 else [mm]))
            mms.append(mm)
        yld_free = mms[-1]
        e1 = p.emit("vector", lambda e: e.tensor_add(attn_sb[:, 0:512], psP1[:], bout_t[:, 0:512]), [mms[0], l_bo])
        e2 = p.emit("vector", lambda e: e.tensor_add(attn_sb[:, 512:1024], psP2[:], bout_t[:, 512:1024]), [mms[1], l_bo])
        ev_free = e2
        lt = emit_ln(p, attn_sb[:], sqs[:], r1[:], r2[:], xn[:], [e1, e2])
        ldx = p.dma("sync", xbuf[:, s, :], x_in[ts, :], [slot_free[s]])
        ta = p.emit("vector", lambda e, s=s: e.tensor_add(outbuf[:, s, :], sqs[:], xbuf[:, s, :]), [lt, ldx, ost_free[s]])
        slot_free[s] = ta
        st = p.dma("gpsimd", xout[ts, :], outbuf[:, s, :], [ta])
        ost_free[s] = st
        st_toks.append(st)
    p.emit("gpsimd", lambda e: e.engine_nop(), st_toks)
    p.build(nc, stack)
    stack.close()
    return nc


def build_block_d():
    nc = bass.Bass("TRN2", target_bir_lowering=False)
    x_in = nc.dram_tensor("x_in", [T, E], F32, kind="ExternalInput").ap()
    W1b_d = nc.dram_tensor("W1b", [32, 128, 8, 128], F32, kind="ExternalInput").ap()
    W2b_d = nc.dram_tensor("W2b", [32, 128, 8, 128], F32, kind="ExternalInput").ap()
    H1b_d = nc.dram_tensor("H1b", [32, 128, 32, 128], F32, kind="ExternalInput").ap()
    H2b_d = nc.dram_tensor("H2b", [32, 128, 32, 128], F32, kind="ExternalInput").ap()
    W3b_d = nc.dram_tensor("W3b", [8, 128, 32, 128], F32, kind="ExternalInput").ap()
    s1_d = nc.dram_tensor("s1t", [128, 32], F32, kind="ExternalInput").ap()
    s2_d = nc.dram_tensor("s2t", [128, 32], F32, kind="ExternalInput").ap()
    sh1_d = nc.dram_tensor("sh1t", [128, 32], F32, kind="ExternalInput").ap()
    sh2_d = nc.dram_tensor("sh2t", [128, 32], F32, kind="ExternalInput").ap()
    s3_d = nc.dram_tensor("s3t", [128, 8], F32, kind="ExternalInput").ap()
    eyeb_d = nc.dram_tensor("eyeb", [128, 128], BF16, kind="ExternalInput").ap()
    xout = nc.dram_tensor("xout", [T, E], F32, kind="ExternalOutput").ap()

    p = Prog()
    stack = contextlib.ExitStack()
    sb = lambda name, shape, dt: stack.enter_context(nc.sbuf_tensor(name, shape, dt))
    _psn = [0]
    def ps(shape, dt):
        _psn[0] += 1
        return stack.enter_context(nc.psum_tensor(f"ps{_psn[0]}", shape, dt))

    seyeb = sb("seyeb", [128, 128], BF16)
    s1t = sb("s1t_s", [128, 32], F32)
    s2t = sb("s2t_s", [128, 32], F32)
    sh1t = sb("sh1t_s", [128, 32], F32)
    sh2t = sb("sh2t_s", [128, 32], F32)
    s3t = sb("s3t_s", [128, 8], F32)
    xnTb = sb("xnTb", [128, ND, 512], BF16)
    bufA = sb("bufA", [128, 32, 512], BF16)
    bufB = sb("bufB", [128, 32, 512], BF16)
    hf = [sb("hf0", [128, 32, 128], F32), sb("hf1", [128, 32, 128], F32)]
    hc = sb("hc", [128, 32, 128], BF16)
    gf = [sb("gf0", [128, 8, 128], F32), sb("gf1", [128, 8, 128], F32)]
    gc = sb("gc", [128, 8, 128], BF16)
    asb = sb("asb", [128, 512], F32)
    bsb = sb("bsb", [128, 512], F32)
    csb2 = sb("csb2", [128, 512], F32)
    stg = sb("stg", [128, 512], BF16)
    tokf = sb("tokf", [128, 4, E], F32)
    xbuf = sb("xbuf", [128, 2, E], F32)
    xn = sb("xn", [128, E], F32)
    xnb = sb("xnb", [128, E], BF16)
    r1 = sb("r1", [128, 1], F32)
    r2 = sb("r2", [128, 1], F32)
    sqs = sb("sqs", [128, E], F32)
    outbuf = sb("outbuf", [128, 2, E], F32)

    psT = ps([128, 128], BF16)
    psG1 = ps([128, 512], F32)
    psG2 = ps([128, 512], F32)
    psH = ps([128, 512], F32)
    psW3 = ps([128, 512], F32)

    l_eye = p.dma("sync", seyeb[:], eyeb_d)
    l_s = [p.dma("sync", t[:], d) for t, d in
           ((s1t, s1_d), (s2t, s2_d), (sh1t, sh1_d), (sh2t, sh2_d), (s3t, s3_d))]

    st_toks = []
    slot_free = [None, None]
    ost_free = [None, None]
    pr = {"ev": [], "last_w2mm": None, "gsl_free": [None, None], "gcast_free": None,
          "psg_free": [None, None], "hsl_free": [None, None], "hcast_free": None,
          "psh_free": None, "asb_used": None, "csb_used": None, "bufA_guard": None,
          "bufB_guard": None, "stg_free": None, "psw3_free": None, "tokf_free": None,
          "w3sl_free": [None, None]}
    for tb in range(8):
        prev_ev = pr["ev"]
        xnt_last = None
        for k in range(4):
            tt = tb * 4 + k
            s = tt % 2
            ld = p.dma("sync", xbuf[:, s, :], x_in[tt * 128:(tt + 1) * 128, :], [slot_free[s]])
            lt, ct, evs = emit_xnt_tile(
                p, tt, xbuf[:, s, :], xn[:], xnb[:],
                lambda dc, k=k: xnTb[:, dc, k * 128:(k + 1) * 128],
                r1[:], r2[:], sqs[:], psT, seyeb[:],
                [ld, l_eye, pr["last_w2mm"]], prev_ev)
            prev_ev = evs
            slot_free[s] = lt
            xnt_last = evs[-1]
        pr["ev"] = prev_ev
        ge_last = None
        for oc in range(32):
            sl = oc % 2
            ld1 = p.dma("sync", gf[sl][:].rearrange("p a b -> p (a b)"),
                        W1b_d[oc].rearrange("p a b -> p (a b)"), [pr["gsl_free"][sl]])
            c1 = p.emit("vector", lambda e, sl=sl: e.tensor_copy(gc[:].rearrange("p a b -> p (a b)"),
                                                                 gf[sl][:].rearrange("p a b -> p (a b)")),
                        [ld1, pr["gcast_free"]])
            mm1 = None
            for dc in range(ND):
                mm1 = p.emit("tensor", lambda e, dc=dc: e.matmul(psG1[:], gc[:, dc, :], xnTb[:, dc, :],
                                                                 start=(dc == 0), stop=(dc == ND - 1)),
                             ([c1, pr["psg_free"][0], xnt_last] if dc == 0 else [mm1]))
            ld2 = p.dma("scalar", gf[sl][:].rearrange("p a b -> p (a b)"),
                        W2b_d[oc].rearrange("p a b -> p (a b)"), [c1])
            c2 = p.emit("vector", lambda e, sl=sl: e.tensor_copy(gc[:].rearrange("p a b -> p (a b)"),
                                                                 gf[sl][:].rearrange("p a b -> p (a b)")),
                        [ld2, mm1])
            pr["gsl_free"][sl] = c2
            mm2 = None
            for dc in range(ND):
                mm2 = p.emit("tensor", lambda e, dc=dc: e.matmul(psG2[:], gc[:, dc, :], xnTb[:, dc, :],
                                                                 start=(dc == 0), stop=(dc == ND - 1)),
                             ([c2, pr["psg_free"][1]] if dc == 0 else [mm2]))
            pr["gcast_free"] = mm2
            pr["last_w2mm"] = mm2
            ta_ = p.emit("vector", lambda e, oc=oc: e.tensor_scalar(asb[:], psG1[:], s1t[:, oc:oc + 1], None, ALU.mult),
                         [mm1, l_s[0], pr["asb_used"]])
            tb_ = p.emit("scalar", lambda e: e.activation(bsb[:], asb[:], AF.Silu), [ta_])
            tc_ = p.emit("vector", lambda e, oc=oc: e.tensor_scalar(csb2[:], psG2[:], s2t[:, oc:oc + 1], None, ALU.mult),
                         [mm2, l_s[1], pr["csb_used"]])
            th_ = p.emit("gpsimd", lambda e, oc=oc: e.tensor_mul(bufA[:, oc, :], bsb[:], csb2[:]),
                         [tb_, tc_, pr["bufA_guard"]])
            pr["psg_free"] = [ta_, tc_]
            pr["asb_used"] = th_
            pr["csb_used"] = th_
            ge_last = th_
        in_last = ge_last
        for phase in range(2):
            wd = H1b_d if phase == 0 else H2b_d
            st_ = sh1t if phase == 0 else sh2t
            hin = bufA if phase == 0 else bufB
            hout = bufB if phase == 0 else bufA
            guard = pr["bufB_guard"] if phase == 0 else pr["last_h1mm"]
            h_last = None
            last_mm = None
            for oc in range(32):
                sl = oc % 2
                ldh = p.dma("sync", hf[sl][:].rearrange("p a b -> p (a b)"),
                            wd[oc].rearrange("p a b -> p (a b)"), [pr["hsl_free"][sl]])
                ch = p.emit("vector", lambda e, sl=sl: e.tensor_copy(hc[:].rearrange("p a b -> p (a b)"),
                                                                     hf[sl][:].rearrange("p a b -> p (a b)")),
                            [ldh, pr["hcast_free"]])
                pr["hsl_free"][sl] = ch
                mm = None
                for dc in range(32):
                    mm = p.emit("tensor", lambda e, dc=dc, hin=hin: e.matmul(psH[:], hc[:, dc, :], hin[:, dc, :],
                                                                             start=(dc == 0), stop=(dc == 31)),
                                ([ch, pr["psh_free"], in_last] if dc == 0 else [mm]))
                pr["hcast_free"] = mm
                last_mm = mm
                ta_ = p.emit("vector", lambda e, oc=oc, st_=st_: e.tensor_scalar(asb[:], psH[:], st_[:, oc:oc + 1],
                                                                                 None, ALU.mult),
                             [mm, pr["asb_used"], l_s[2]])
                pr["psh_free"] = ta_
                tg_ = p.emit("scalar", lambda e, oc=oc, hout=hout: e.activation(hout[:, oc, :], asb[:], AF.Gelu),
                             [ta_] + ([guard] if oc == 0 else []))
                pr["asb_used"] = tg_
                h_last = tg_
            in_last = h_last
            if phase == 0:
                pr["last_h1mm"] = last_mm
        trp = pr["ev"][-1] if pr["ev"] else None
        last_w3mm = None
        for oc3 in range(8):
            sl = oc3 % 2
            ld3 = p.dma("sync", hf[sl][:].rearrange("p a b -> p (a b)"),
                        W3b_d[oc3].rearrange("p a b -> p (a b)"), [pr["hsl_free"][sl]])
            c3 = p.emit("vector", lambda e, sl=sl: e.tensor_copy(hc[:].rearrange("p a b -> p (a b)"),
                                                                 hf[sl][:].rearrange("p a b -> p (a b)")),
                        [ld3, pr["hcast_free"]])
            pr["hsl_free"][sl] = c3
            mm = None
            for dc in range(32):
                mm = p.emit("tensor", lambda e, dc=dc: e.matmul(psW3[:], hc[:, dc, :], bufA[:, dc, :],
                                                                start=(dc == 0), stop=(dc == 31)),
                            ([c3, pr["psw3_free"], in_last] if dc == 0 else [mm]))
            pr["hcast_free"] = mm
            last_w3mm = mm
            tsc = p.emit("vector", lambda e, oc3=oc3: e.tensor_scalar(stg[:], psW3[:], s3t[:, oc3:oc3 + 1],
                                                                      None, ALU.mult),
                         [mm, l_s[4], pr["stg_free"]])
            pr["psw3_free"] = tsc
            trv = None
            for k in range(4):
                trv = p.emit("tensor",
                             lambda e, k=k: e.transpose(psT[:, 0:128], stg[:, k * 128:(k + 1) * 128], seyeb[:]),
                             [tsc] + ([trv] if trv else [trp]))
                evv = p.emit("scalar",
                             lambda e, k=k, oc3=oc3: e.copy(tokf[:, k, oc3 * 128:(oc3 + 1) * 128], psT[:, 0:128]),
                             [trv, pr["tokf_free"]])
                trv = evv
            trp = trv
            pr["stg_free"] = trv
        pr["bufA_guard"] = last_w3mm
        pr["bufB_guard"] = pr["last_h1mm"]
        pr["bufB_guard"] = pr["hcast_free"] if False else pr["bufB_guard"]
        pr["bufB_guard"] = last_w3mm
        tok_last = None
        for k in range(4):
            tt = tb * 4 + k
            s = tt % 2
            ts = slice(tt * 128, (tt + 1) * 128)
            lt = emit_ln(p, tokf[:, k, :], xn[:], r1[:], r2[:], sqs[:], [pr["stg_free"], slot_free[s]])
            ldx = p.dma("sync", xbuf[:, s, :], x_in[ts, :], [slot_free[s], lt])
            ta = p.emit("vector", lambda e, s=s, k=k: e.tensor_add(outbuf[:, s, :], xn[:], xbuf[:, s, :]),
                        [lt, ldx, ost_free[s]])
            slot_free[s] = ta
            st = p.dma("gpsimd", xout[ts, :], outbuf[:, s, :], [ta])
            ost_free[s] = st
            st_toks.append(st)
            tok_last = ta
        pr["tokf_free"] = tok_last
    p.emit("gpsimd", lambda e: e.engine_nop(), st_toks)
    p.build(nc, stack)
    stack.close()
    return nc


_CACHE = {}


def _get(name, builder):
    if name not in _CACHE:
        _CACHE[name] = builder()
    return _CACHE[name]


def _prep(inputs):
    if "prep" in _CACHE:
        return _CACHE["prep"]
    f32 = np.float32
    c = np.ascontiguousarray
    pr = {}
    eyeb = np.eye(128, dtype=BF)
    eyef = np.eye(128, dtype=f32)
    waT = c(inputs["L0_attn_w_attn"].T)
    whead = np.stack([np.concatenate([waT[:, h * 64:(h + 1) * 64],
                                      waT[:, 1024 + h * 64:1024 + (h + 1) * 64],
                                      waT[:, 2048 + h * 64:2048 + (h + 1) * 64]], axis=1)
                      for h in range(16)])
    pr["whead"] = c(whead.reshape(16, 8, 128, 192).transpose(0, 2, 1, 3))
    pr["wpT"] = c(inputs["L0_attn_w_proj"].T)
    dm = np.zeros((4, 128, 512), dtype=f32)
    jl = np.arange(128)[:, None]
    il = np.arange(512)[None, :]
    for pp in range(4):
        dm[pp] = (il >= jl + pp * 128).astype(f32)
    pr["dmask"] = dm.astype(BF)
    pr["eyeb"] = eyeb
    pr["eyef"] = eyef
    pr["knots"] = c(np.tile(inputs["L0_ffn_knot_values"][None, :], (128, 1)))
    pr["temp"] = c(np.tile(inputs["L0_ffn_temperature"][None, :], (128, 1)))
    pr["cw1"] = c(inputs["L0_ffn_compress_w1"])
    pr["cb1"] = c(np.tile(inputs["L0_ffn_compress_b1"][None, :], (128, 1)))
    pr["cw2"] = c(inputs["L0_ffn_compress_w2"].reshape(2, 128, 2))
    pr["cb2"] = c(np.tile(inputs["L0_ffn_compress_b2"][None, :], (128, 1)))
    pr["spl"] = c(np.concatenate([inputs["L0_ffn_spline_w1"][:, 0, :],
                                  inputs["L0_ffn_spline_w1"][:, 1, :],
                                  inputs["L0_ffn_spline_b1"],
                                  inputs["L0_ffn_spline_w2"][:, :, 0],
                                  inputs["L0_ffn_spline_b2"]], axis=1))
    pr["dirs"] = c(inputs["L0_ffn_directions"])
    pr["gam"] = c(np.tile(inputs["L0_ffn_norm_gamma"][None, :], (128, 1)))
    pr["bet"] = c(np.tile(inputs["L0_ffn_norm_beta"][None, :], (128, 1)))
    pr["osc"] = c(np.tile(inputs["L0_ffn_output_scale"][None, :], (128, 1)))
    a8 = np.zeros((64, 8), dtype=f32)
    for cc in range(8):
        a8[cc * 8:(cc + 1) * 8, cc] = 1.0
    pr["a8"] = a8
    pr["w8"] = c(np.tile(np.arange(8, 0, -1, dtype=f32)[None, :], (128, 1)))
    pr["w64"] = c(np.tile(np.arange(64, 0, -1, dtype=f32)[None, :], (128, 1)))
    pr["sraw"] = c(inputs["L0_ffn_signatures_raw"])
    wq = inputs["L1_attn_w_qkv"]
    whd = np.stack([np.concatenate([wq[:, h * 64:(h + 1) * 64],
                                    wq[:, 1024 + h * 64:1024 + (h + 1) * 64],
                                    wq[:, 2048 + h * 64:2048 + (h + 1) * 64]], axis=1)
                    for h in range(16)])
    pr["whd"] = c(whd.reshape(16, 8, 128, 192).transpose(0, 2, 1, 3))
    bq = inputs["L1_attn_b_qkv"]
    bqk = np.zeros((128, 16), dtype=f32)
    bv = np.zeros((64, 16), dtype=f32)
    for h in range(16):
        bqk[0:64, h] = bq[h * 64:(h + 1) * 64]
        bqk[64:128, h] = bq[1024 + h * 64:1024 + (h + 1) * 64]
        bv[:, h] = bq[2048 + h * 64:2048 + (h + 1) * 64]
    pr["bqk"] = bqk
    pr["bv"] = bv
    pr["wout"] = c(inputs["L1_attn_w_out"])
    pr["bout"] = c(np.tile(inputs["L1_attn_b_out"][None, :], (128, 1)))
    half = 32
    freqs = (np.arange(half, dtype=f32) / f32(half)).astype(f32)
    theta = (1.0 / (10000.0 ** freqs)).astype(f32)
    pos = np.arange(T, dtype=f32)
    ang = (pos[:, None] * theta[None, :]).astype(f32)
    cosv = np.cos(ang).astype(f32)
    sinv = np.sin(ang).astype(f32)
    cosT = np.zeros((64, T), dtype=f32)
    sinT = np.zeros((64, T), dtype=f32)
    cosT[0::2, :] = cosv.T
    cosT[1::2, :] = cosv.T
    sinT[0::2, :] = -sinv.T
    sinT[1::2, :] = sinv.T
    pr["cosT"] = cosT
    pr["sinT"] = sinT
    perm = np.zeros((64, 64), dtype=f32)
    for j in range(32):
        perm[2 * j, 2 * j + 1] = 1.0
        perm[2 * j + 1, 2 * j] = 1.0
    pr["perm"] = perm
    def wblocks(w, noc, ndc):
        wT = w.T
        nin = wT.shape[0]
        return c(wT.reshape(ndc, 128, noc, 128).transpose(2, 1, 0, 3))
    pr["W1b"] = wblocks(inputs["L1_ffn_w1_w"], 32, 8)
    pr["W2b"] = wblocks(inputs["L1_ffn_w2_w"], 32, 8)
    pr["H1b"] = wblocks(inputs["L1_ffn_h1_w"], 32, 32)
    pr["H2b"] = wblocks(inputs["L1_ffn_h2_w"], 32, 32)
    pr["W3b"] = wblocks(inputs["L1_ffn_w3_w"], 8, 32)
    pr["s1t"] = c(inputs["L1_ffn_w1_s"].reshape(32, 128).T)
    pr["s2t"] = c(inputs["L1_ffn_w2_s"].reshape(32, 128).T)
    pr["sh1t"] = c(inputs["L1_ffn_h1_s"].reshape(32, 128).T)
    pr["sh2t"] = c(inputs["L1_ffn_h2_s"].reshape(32, 128).T)
    pr["s3t"] = c(inputs["L1_ffn_w3_s"].reshape(8, 128).T)
    _CACHE["prep"] = pr
    return pr


def kernel(**inputs):
    pr = _prep(inputs)
    x = np.ascontiguousarray(inputs["L0_attn_x"].astype(np.float32))
    cores = list(range(8))

    nc_a = _get("a", build_block_a)
    maps = [dict(x_in=x[c], whead=pr["whead"], wpT=pr["wpT"], dmask=pr["dmask"], eyeb=pr["eyeb"])
            for c in cores]
    res = run_bass_kernel_spmd(nc_a, maps, core_ids=cores)
    x1 = [res.results[c]["xout"] for c in cores]

    nc_b = _get("b", build_block_b)
    maps = [dict(x_in=x1[c], sraw=pr["sraw"], knots=pr["knots"], temp=pr["temp"], cw1=pr["cw1"],
                 cb1=pr["cb1"], cw2=pr["cw2"], cb2=pr["cb2"], spl=pr["spl"], dirs=pr["dirs"],
                 gam=pr["gam"], bet=pr["bet"], osc=pr["osc"], a8=pr["a8"], w8=pr["w8"],
                 w64=pr["w64"], eyef=pr["eyef"], eyeb=pr["eyeb"]) for c in cores]
    res = run_bass_kernel_spmd(nc_b, maps, core_ids=cores)
    x2 = [res.results[c]["xout"] for c in cores]

    nc_c = _get("c", build_block_c)
    maps = [dict(x_in=x2[c], whd=pr["whd"], bqk=pr["bqk"], bv=pr["bv"], wout=pr["wout"],
                 bout=pr["bout"], cosT=pr["cosT"], sinT=pr["sinT"], perm=pr["perm"],
                 eyeb=pr["eyeb"]) for c in cores]
    res = run_bass_kernel_spmd(nc_c, maps, core_ids=cores)
    x3 = [res.results[c]["xout"] for c in cores]

    nc_d = _get("d", build_block_d)
    maps = [dict(x_in=x3[c], W1b=pr["W1b"], W2b=pr["W2b"], H1b=pr["H1b"], H2b=pr["H2b"],
                 W3b=pr["W3b"], s1t=pr["s1t"], s2t=pr["s2t"], sh1t=pr["sh1t"], sh2t=pr["sh2t"],
                 s3t=pr["s3t"], eyeb=pr["eyeb"]) for c in cores]
    res = run_bass_kernel_spmd(nc_d, maps, core_ids=cores)
    x4 = [res.results[c]["xout"] for c in cores]

    return np.stack(x4).astype(np.float32)


# version 02 (final)
# speedup vs baseline: 1.0012x; 1.0012x over previous
import contextlib
import numpy as np
import ml_dtypes
import concourse.bass as bass
import concourse.mybir as mybir
from concourse.bass_utils import run_bass_kernel_spmd

F32 = mybir.dt.float32
BF16 = mybir.dt.bfloat16
AF = mybir.ActivationFunctionType
ALU = mybir.AluOpType
AX = mybir.AxisListType
BF = ml_dtypes.bfloat16

ENGINES = ["tensor", "vector", "scalar", "gpsimd", "sync"]

T = 4096
E = 1024
NT = 32
ND = 8
NH = 16
HD = 64


class Tok:
    __slots__ = ("eng", "val")
    def __init__(self, eng, val):
        self.eng = eng
        self.val = val


class Prog:
    def __init__(self):
        self.ops = {e: [] for e in ENGINES}
        self.count = {e: 0 for e in ENGINES}
        self.default_deps = []

    def emit(self, eng, fn, deps=(), inc=1):
        self.ops[eng].append((fn, [d for d in list(deps) + self.default_deps if d is not None], inc))
        self.count[eng] += inc
        return Tok(eng, self.count[eng])

    def dma(self, eng, out, in_, deps=()):
        return self.emit(eng, lambda e, out=out, in_=in_: e.dma_start(out=out, in_=in_),
                         deps, inc=16)

    def barrier(self):
        return [Tok(e, self.count[e]) for e in ENGINES if self.count[e] > 0]

    def build(self, nc, stack):
        sems = {e: stack.enter_context(nc.semaphore(f"sem_{e}")) for e in ENGINES}
        block = stack.enter_context(nc.Block())

        def make_section(eng):
            def body(engine):
                last = {e: 0 for e in ENGINES}
                for fn, deps, inc in self.ops[eng]:
                    for d in deps:
                        if d.eng == eng:
                            if d.val > last[eng]:
                                engine.wait_ge(sems[eng], d.val)
                                last[eng] = d.val
                        elif d.val > last[d.eng]:
                            engine.wait_ge(sems[d.eng], d.val)
                            last[d.eng] = d.val
                    fn(engine).then_inc(sems[eng], inc)
            return body

        for eng in ENGINES:
            getattr(block, eng)(make_section(eng))


def emit_ln(p, src_ap, out_ap, r1, r2, sqs, deps, W=1024, gamma=None, beta=None, gb_scr=None):
    t1 = p.emit("vector", lambda e: e.reduce_sum(r1, src_ap, axis=AX.X), deps)
    t2 = p.emit("vector", lambda e: e.tensor_scalar(r1, r1, 1.0 / W, None, ALU.mult), [t1])
    t3 = p.emit("vector", lambda e: e.tensor_scalar(out_ap, src_ap, r1, None, ALU.subtract), [t2] + list(deps))
    t4 = p.emit("scalar", lambda e: e.activation(sqs, out_ap, AF.Square), [t3])
    t5 = p.emit("vector", lambda e: e.reduce_sum(r2, sqs, axis=AX.X), [t4])
    t6 = p.emit("vector", lambda e: e.tensor_scalar(r2, r2, 1.0 / W, 1e-5, ALU.mult, ALU.add), [t5])
    t7 = p.emit("scalar", lambda e: e.activation(r2, r2, AF.Sqrt), [t6])
    t8 = p.emit("vector", lambda e: e.reciprocal(r2, r2), [t7])
    t9 = p.emit("vector", lambda e: e.tensor_scalar(out_ap, out_ap, r2, None, ALU.mult), [t8])
    if gamma is not None:
        t10 = p.emit("vector", lambda e: e.tensor_mul(out_ap, out_ap, gamma), [t9])
        t9 = p.emit("vector", lambda e: e.tensor_add(out_ap, out_ap, beta), [t10])
    return t9


def emit_xnt_tile(p, tt, xsrc_ap, xn, xnb, xnT_slices, r1, r2, sqs, psT, seyeb, deps, prev_evacs):
    lt = emit_ln(p, xsrc_ap, xn, r1, r2, sqs, deps)
    ct = p.emit("vector", lambda e: e.tensor_copy(xnb, xn), [lt])
    evacs = []
    tprev = prev_evacs
    for dc in range(ND):
        tr = p.emit("tensor",
                    lambda e, dc=dc: e.transpose(psT[:, 0:128], xnb[:, dc * 128:(dc + 1) * 128], seyeb),
                    [ct] + ([tprev[-1]] if tprev else []))
        ev = p.emit("vector",
                    lambda e, dc=dc: e.tensor_copy(xnT_slices(dc), psT[:, 0:128]), [tr])
        tprev = [ev]
        evacs.append(ev)
    return lt, ct, evacs


def build_block_a(ctx=None):
    if ctx is None:
        nc = bass.Bass("TRN2", target_bir_lowering=False)
        p = Prog()
        pfx = ""
        x_io = None
    else:
        nc, p, pfx, x_io = ctx
    if x_io is None:
        x_in = nc.dram_tensor("x_in", [T, E], F32, kind="ExternalInput").ap()
        xout = nc.dram_tensor("xout", [T, E], F32, kind="ExternalOutput").ap()
    else:
        x_in, xout = x_io
    whead = nc.dram_tensor(pfx + "whead", [NH, 128, ND, 192], F32, kind="ExternalInput").ap()
    wpT = nc.dram_tensor(pfx + "wpT", [E, E], F32, kind="ExternalInput").ap()
    dmask = nc.dram_tensor(pfx + "dmask", [4, 128, 512], BF16, kind="ExternalInput").ap()
    eyeb_d = nc.dram_tensor(pfx + "eyeb", [128, 128], BF16, kind="ExternalInput").ap()

    stack = contextlib.ExitStack()
    sb = lambda name, shape, dt: stack.enter_context(nc.sbuf_tensor(pfx + name, shape, dt))
    _psn = [0]
    def ps(shape, dt):
        _psn[0] += 1
        return stack.enter_context(nc.psum_tensor(f"{pfx}ps{_psn[0]}", shape, dt))

    xnT = sb("xnT", [128, ND, T], BF16)
    yT = sb("yT", [128, ND, T], BF16)
    seyeb = sb("seyeb", [128, 128], BF16)
    onesf = sb("onesf", [1, 64], F32)
    xbuf = sb("xbuf", [128, 2, E], F32)
    xn = sb("xn", [128, E], F32)
    xnb = sb("xnb", [128, E], BF16)
    r1 = sb("r1", [128, 1], F32)
    r2 = sb("r2", [128, 1], F32)
    sqs = sb("sqs", [128, E], F32)
    wpbuf = sb("wpbuf", [128, E], F32)
    wpTb = sb("wpTb", [128, ND, E], BF16)
    attn_sb = sb("attn_sb", [128, E], F32)
    outbuf = sb("outbuf", [128, E], F32)
    stack2 = contextlib.ExitStack()
    sb2 = lambda name, shape, dt: stack2.enter_context(nc.sbuf_tensor(pfx + "2" + name, shape, dt))
    masks = sb2("masks", [128, 4, 512], BF16)
    vo = sb2("vo", [128, NT, 65], BF16)
    vtb = xnb[0:64, 0:512]
    phiq = sb2("phiq", [64, T], BF16)
    phik = sb2("phik", [64, T], BF16)
    whbuf = sb2("whbuf", [128, 192], F32)
    whb = sb2("whb", [128, ND, 192], BF16)
    Wbuf = sb2("Wbuf", [128, 2, 512], BF16)
    dscr = xn[0:1, 512:1024]
    drcp = xbuf[0:1, 0, 0:512]
    numsb = xbuf[0:64, 1, 0:512]
    mn = sqs[:, 0:512]
    ex = sqs[:, 512:1024]
    rx = xn[:, 0:512]

    psT = ps([128, 128], BF16)
    psQK = ps([128, 512], F32)
    psA = ps([128, 512], F32)
    psO = ps([128, 512], F32)
    psB2 = ps([64, 512], F32)
    psP1 = ps([128, 512], F32)
    psP2 = ps([128, 512], F32)

    ld_eye = p.dma("sync", seyeb[:], eyeb_d)
    ld_mask = p.dma("sync", masks[:], dmask.rearrange("k p f -> p k f"))
    t_ones = p.emit("vector", lambda e: e.memset(onesf[:], 1.0))
    t_voones = p.emit("vector", lambda e: e.memset(vo[:, :, 64:65], 1.0))

    slot_free = [None, None]
    prev_ev = []
    for tt in range(NT):
        s = tt % 2
        ld = p.dma("sync", xbuf[:, s, :], x_in[tt * 128:(tt + 1) * 128, :],
                   [slot_free[s]])
        lt, ct, evs = emit_xnt_tile(
            p, tt, xbuf[:, s, :], xn[:], xnb[:],
            lambda dc, tt=tt: xnT[:, dc, tt * 128:(tt + 1) * 128],
            r1[:], r2[:], sqs[:], psT, seyeb[:], [ld, ld_eye], prev_ev)
        prev_ev = evs
        slot_free[s] = lt

    bar1 = p.barrier()

    w_free = None
    psqk_free = None
    psA_free = None
    psO_free = None
    psB2_free = None
    wslot_free = [None, None]
    vtb_free = None
    for h in range(NH):
        hp0 = (h % 2) * 64
        hdc = h // 2
        cw = None
        for dc in range(ND):
            ldw = p.dma("sync", whbuf[:], whead[h, :, dc, :], [w_free] + (bar1 if h == 0 else []))
            cw = p.emit("vector", lambda e, dc=dc: e.tensor_copy(whb[:, dc, :], whbuf[:]), [ldw, cw])
            w_free = cw
        phi_toks = []
        for tg in range(8):
            gs = slice(tg * 512, (tg + 1) * 512)
            mm = None
            for dc in range(ND):
                mm = p.emit("tensor",
                            lambda e, dc=dc, gs=gs: e.matmul(psQK[:], whb[:, dc, 0:128],
                                                             xnT[:, dc, gs],
                                                             start=(dc == 0), stop=(dc == ND - 1)),
                            ([cw, psqk_free] if dc == 0 else [mm]))
            last_rx = None
            for b0, dst in ((0, phiq), (64, phik)):
                mnh = sqs[b0:b0 + 64, 0:512]
                exh = sqs[b0:b0 + 64, 512:1024]
                rxh = xn[b0:b0 + 64, 0:512]
                t_mn = p.emit("vector", lambda e, b0=b0, mnh=mnh: e.tensor_scalar(mnh, psQK[b0:b0 + 64, :], 0.0, None, ALU.min), [mm])
                t_ex = p.emit("scalar", lambda e, mnh=mnh, exh=exh: e.activation(exh, mnh, AF.Exp), [t_mn])
                t_rx = p.emit("vector", lambda e, b0=b0, rxh=rxh: e.tensor_scalar(rxh, psQK[b0:b0 + 64, :], 0.0, None, ALU.max), [mm, t_mn])
                t_phi = p.emit("vector", lambda e, gs=gs, dst=dst, exh=exh, rxh=rxh: e.tensor_add(dst[:, gs], exh, rxh), [t_ex, t_rx])
                last_rx = t_rx
                phi_toks.append(t_phi)
            psqk_free = t_phi
        for tg in range(8):
            gs = slice(tg * 512, (tg + 1) * 512)
            mm = None
            for dc in range(ND):
                mm = p.emit("tensor",
                            lambda e, dc=dc, gs=gs: e.matmul(psQK[0:64, :], whb[:, dc, 128:192],
                                                             xnT[:, dc, gs],
                                                             start=(dc == 0), stop=(dc == ND - 1)),
                            ([psqk_free] if dc == 0 else [mm]))
            cv = p.emit("scalar", lambda e: e.copy(vtb, psQK[0:64, :]), [mm, vtb_free])
            psqk_free = cv
            trv = None
            for k in range(4):
                jc = tg * 4 + k
                trv = p.emit("tensor",
                             lambda e, k=k: e.transpose(psT[:, 0:64], vtb[:, k * 128:(k + 1) * 128], seyeb[0:64, 0:64]),
                             [cv] + ([trv] if trv else [prev_ev[-1] if prev_ev else None]))
                evv = p.emit("vector",
                             lambda e, jc=jc: e.tensor_copy(vo[:, jc, 0:64], psT[:, 0:64]),
                             [trv, t_voones])
                trv = evv
            vtb_free = trv
        for g in range(8):
            gs = slice(g * 512, (g + 1) * 512)
            njc = 4 * g + 4
            mmo_last = None
            for jc in range(njc):
                js = slice(jc * 128, (jc + 1) * 128)
                mmS = p.emit("tensor",
                             lambda e, js=js, gs=gs: e.matmul(psA[:], phik[:, js], phiq[:, gs],
                                                              start=True, stop=True, skip_group_check=True),
                             [psA_free] + (phi_toks if jc == 0 else []))
                wsl = jc % 2
                weng = "vector" if (jc % 2 == 0) else "scalar"
                if jc >= 4 * g:
                    mp = jc - 4 * g
                    wt = p.emit("vector",
                                lambda e, wsl=wsl, mp=mp: e.tensor_mul(Wbuf[:, wsl, :], psA[:], masks[:, mp, :]),
                                [mmS, wslot_free[wsl], ld_mask])
                else:
                    if weng == "vector":
                        wt = p.emit("vector", lambda e, wsl=wsl: e.tensor_copy(Wbuf[:, wsl, :], psA[:]),
                                    [mmS, wslot_free[wsl]])
                    else:
                        wt = p.emit("scalar", lambda e, wsl=wsl: e.copy(Wbuf[:, wsl, :], psA[:]),
                                    [mmS, wslot_free[wsl]])
                psA_free = wt
                mmo_last = p.emit("tensor",
                                  lambda e, jc=jc, wsl=wsl, njc=njc: e.matmul(
                                      psO[0:65, :], vo[:, jc, :], Wbuf[:, wsl, :],
                                      start=(jc == 0), stop=(jc == njc - 1), skip_group_check=True),
                                  [wt] + ([psO_free] if jc == 0 else []))
                wslot_free[wsl] = mmo_last
            td = p.emit("vector", lambda e: e.tensor_scalar(dscr, psO[64:65, :], 1e-6, None, ALU.add), [mmo_last])
            tr_ = p.emit("vector", lambda e: e.reciprocal(drcp, dscr), [td])
            tb2 = p.emit("tensor", lambda e: e.matmul(psB2[:], onesf[:], drcp, start=True, stop=True,
                                                      skip_group_check=True),
                         [tr_, t_ones, psB2_free])
            tnum = p.emit("scalar", lambda e: e.copy(numsb, psO[0:64, :]), [mmo_last])
            ty = p.emit("vector",
                        lambda e, hp0=hp0, hdc=hdc, gs=gs: e.tensor_mul(yT[hp0:hp0 + 64, hdc, gs],
                                                                        numsb, psB2[:]),
                        [tnum, tb2])
            psO_free = ty
            psB2_free = ty

    bar2 = p.barrier()
    stack2.close()

    cwp = None
    for dc in range(ND):
        ldp = p.dma("sync", wpbuf[:], wpT[dc * 128:(dc + 1) * 128, :], bar2 + [cwp])
        cwp = p.emit("vector", lambda e, dc=dc: e.tensor_copy(wpTb[:, dc, :], wpbuf[:]), [ldp])
    slot_free = [None, None]
    ost_free = [None, None]
    ev_free = None
    st_toks = []
    for tt in range(NT):
        s = tt % 2
        ts = slice(tt * 128, (tt + 1) * 128)
        mms = []
        for oh, psP in ((0, psP1), (1, psP2)):
            mm = None
            for dc in range(ND):
                mm = p.emit("tensor",
                            lambda e, dc=dc, oh=oh, psP=psP, ts=ts: e.matmul(
                                psP[:], yT[:, dc, ts], wpTb[:, dc, oh * 512:(oh + 1) * 512],
                                start=(dc == 0), stop=(dc == ND - 1)),
                            ([cwp, ev_free] if dc == 0 else [mm]))
            mms.append(mm)
        e1 = p.emit("scalar", lambda e: e.copy(attn_sb[:, 0:512], psP1[:]), [mms[0]])
        e2 = p.emit("scalar", lambda e: e.copy(attn_sb[:, 512:1024], psP2[:]), [mms[1]])
        ev_free = e2
        lt = emit_ln(p, attn_sb[:], sqs[:] , r1[:], r2[:], xn[:], [e1, e2])
        ldx = p.dma("sync", xbuf[:, s, :], x_in[ts, :], [slot_free[s]])
        ta = p.emit("vector", lambda e, s=s: e.tensor_add(outbuf[:], sqs[:], xbuf[:, s, :]), [lt, ldx, ost_free[0]])
        slot_free[s] = ta
        st = p.dma("gpsimd", xout[ts, :], outbuf[:], [ta])
        ost_free[0] = st
        st_toks.append(st)
    p.emit("gpsimd", lambda e: e.engine_nop(), st_toks)
    if ctx is None:
        p.build(nc, stack)
        stack.close()
        return nc
    stack.close()
    return None


def build_block_b(ctx=None):
    if ctx is None:
        nc = bass.Bass("TRN2", target_bir_lowering=False)
        p = Prog()
        pfx = ""
        x_io = None
    else:
        nc, p, pfx, x_io = ctx
    if x_io is None:
        x_in = nc.dram_tensor("x_in", [T, E], F32, kind="ExternalInput").ap()
        xout = nc.dram_tensor("xout", [T, E], F32, kind="ExternalOutput").ap()
    else:
        x_in, xout = x_io
    sraw = nc.dram_tensor(pfx + "sraw", [64, E], F32, kind="ExternalInput").ap()
    knots_d = nc.dram_tensor(pfx + "knots", [128, 8], F32, kind="ExternalInput").ap()
    temp_d = nc.dram_tensor(pfx + "temp", [128, 1], F32, kind="ExternalInput").ap()
    cw1_d = nc.dram_tensor(pfx + "cw1", [E, 256], F32, kind="ExternalInput").ap()
    cb1_d = nc.dram_tensor(pfx + "cb1", [128, 256], F32, kind="ExternalInput").ap()
    cw2_d = nc.dram_tensor(pfx + "cw2", [2, 128, 2], F32, kind="ExternalInput").ap()
    cb2_d = nc.dram_tensor(pfx + "cb2", [128, 2], F32, kind="ExternalInput").ap()
    spl_d = nc.dram_tensor(pfx + "spl", [64, 65], F32, kind="ExternalInput").ap()
    dirs_d = nc.dram_tensor(pfx + "dirs", [64, E], F32, kind="ExternalInput").ap()
    gam_d = nc.dram_tensor(pfx + "gam", [128, E], F32, kind="ExternalInput").ap()
    bet_d = nc.dram_tensor(pfx + "bet", [128, E], F32, kind="ExternalInput").ap()
    osc_d = nc.dram_tensor(pfx + "osc", [128, 1], F32, kind="ExternalInput").ap()
    a8_d = nc.dram_tensor(pfx + "a8", [64, 8], F32, kind="ExternalInput").ap()
    w8_d = nc.dram_tensor(pfx + "w8", [128, 8], F32, kind="ExternalInput").ap()
    w64_d = nc.dram_tensor(pfx + "w64", [128, 64], F32, kind="ExternalInput").ap()
    eyef_d = nc.dram_tensor(pfx + "eyef", [128, 128], F32, kind="ExternalInput").ap()
    eyeb_d = nc.dram_tensor(pfx + "eyeb", [128, 128], BF16, kind="ExternalInput").ap()

    stack = contextlib.ExitStack()
    sb = lambda name, shape, dt: stack.enter_context(nc.sbuf_tensor(pfx + name, shape, dt))
    _psn = [0]
    def ps(shape, dt):
        _psn[0] += 1
        return stack.enter_context(nc.psum_tensor(f"{pfx}ps{_psn[0]}", shape, dt))

    sigall = sb("sigall", [128, E], F32)
    q1 = sb("q1", [64, E], F32)
    q2 = sb("q2", [64, E], F32)
    sa8 = sb("sa8", [8, 512], F32)
    sb8 = sb("sb8", [8, 512], F32)
    scat = sb("scat", [128, ND, 72], F32)
    a8s = sb("a8s", [64, 8], F32)
    knots = sb("knots_s", [128, 8], F32)
    dks = sb("dks", [128, 7], F32)
    invt = sb("invt", [128, 1], F32)
    cw1b = sb("cw1b", [128, ND, 256], BF16)
    cw1f = sb("cw1f", [128, ND, 256], F32)
    cb1t = sb("cb1t", [128, 256], F32)
    cw2b = sb("cw2b", [128, 2, 2], BF16)
    cw2f = sb("cw2f", [128, 2, 2], F32)
    cb2t = sb("cb2t", [128, 2], F32)
    spl = sb("spl_s", [64, 65], F32)
    dirs = sb("dirs_s", [64, E], F32)
    gam = sb("gam_s", [128, E], F32)
    bet = sb("bet_s", [128, E], F32)
    osc = sb("osc_s", [128, 1], F32)
    w8t = sb("w8t", [128, 8], F32)
    w64t = sb("w64t", [128, 64], F32)
    eyef = sb("eyef_s", [128, 128], F32)
    eyeb = sb("eyeb_s", [128, 128], BF16)
    xbuf = sb("xbuf", [128, 2, E], F32)
    xn = sb("xn", [128, E], F32)
    r1 = sb("r1", [128, 1], F32)
    r2 = sb("r2", [128, 1], F32)
    sqs = sb("sqs", [128, E], F32)
    xnTf = sb("xnTf", [128, ND, 128], F32)
    xnTb = sb("xnTb", [128, ND, 128], BF16)
    fsb = sb("fsb", [128, 72], F32)
    gsb = sb("gsb", [128, 72], F32)
    csb = sb("csb", [128, 72], F32)
    e8 = sb("e8", [128, 8], F32)
    oh8 = sb("oh8", [128, 8], F32)
    oh64 = sb("oh64", [128, 64], F32)
    tm64 = sb("tm64", [128, 64], F32)
    msb = sb("msb", [128, 64], F32)
    e64 = sb("e64", [128, 64], F32)
    ohsb = sb("ohsb", [128, 64], F32)
    ohT = sb("ohT", [64, 128], F32)
    hgel = sb("hgel", [128, 256], F32)
    hgb = sb("hgb", [128, 256], BF16)
    hTb = sb("hTb", [128, 2, 128], BF16)
    comp = sb("comp", [128, 2], F32)
    Wsel = sb("Wsel", [128, 65], F32)
    tsp = sb("tsp", [128, 16], F32)
    hhs = sb("hhs", [128, 16], F32)
    magv = sb("magv", [128, 1], F32)
    ysb = sb("ysb", [128, E], F32)
    zsb = sb("zsb", [128, E], F32)
    outbuf = sb("outbuf", [128, 2, E], F32)

    psTf = ps([128, 128], F32)
    psTb = ps([128, 128], BF16)
    psS = ps([128, 128], F32)
    psH = ps([128, 256], F32)
    psC2 = ps([128, 16], F32)
    psW = ps([128, 128], F32)
    psD1 = ps([128, 512], F32)
    psD2 = ps([128, 512], F32)

    l_eye = p.dma("sync", eyef[:], eyef_d)
    l_eyb = p.dma("sync", eyeb[:], eyeb_d)
    l_sr = p.dma("sync", sigall[0:64, :], sraw)
    l_a8 = p.dma("sync", a8s[:], a8_d)
    l_kn = p.dma("sync", knots[:], knots_d)
    l_tp = p.dma("sync", invt[:], temp_d)
    l_cw1 = p.dma("sync", cw1f[:], cw1_d.rearrange("(a p) b -> p a b", p=128))
    l_cb1 = p.dma("sync", cb1t[:], cb1_d)
    l_cw2 = p.dma("sync", cw2f[:], cw2_d.rearrange("a p b -> p a b"))
    l_cb2 = p.dma("sync", cb2t[:], cb2_d)
    l_spl = p.dma("sync", spl[:], spl_d)
    l_dir = p.dma("sync", dirs[:], dirs_d)
    l_gam = p.dma("sync", gam[:], gam_d)
    l_bet = p.dma("sync", bet[:], bet_d)
    l_osc = p.dma("sync", osc[:], osc_d)
    l_w8 = p.dma("sync", w8t[:], w8_d)
    l_w64 = p.dma("sync", w64t[:], w64_d)

    tq1 = p.emit("vector", lambda e: e.tensor_scalar(q1[:], sigall[0:64, :], 0.3, None, ALU.is_gt), [l_sr])
    tq2 = p.emit("vector", lambda e: e.tensor_scalar(q2[:], sigall[0:64, :], -0.3, None, ALU.is_lt), [l_sr])
    tq = p.emit("vector", lambda e: e.tensor_sub(sigall[0:64, :], q1[:], q2[:]), [tq1, tq2])
    tcs = []
    for half in range(2):
        hs = slice(half * 512, (half + 1) * 512)
        m1 = p.emit("tensor", lambda e, hs=hs: e.matmul(psD1[0:8, :], a8s[:], sigall[0:64, hs],
                                                        start=True, stop=True), [tq, l_a8] + tcs)
        s1 = p.emit("vector", lambda e: e.tensor_scalar(sa8[:], psD1[0:8, :], 0.0, None, ALU.is_gt), [m1])
        s2 = p.emit("vector", lambda e: e.tensor_scalar(sb8[:], psD1[0:8, :], 0.0, None, ALU.is_lt), [m1])
        s3 = p.emit("vector", lambda e, hs=hs: e.tensor_sub(sigall[64:72, hs], sa8[:], sb8[:]), [s1, s2])
        tcs = [s3]
    trp = None
    for dc in range(ND):
        tr = p.emit("tensor", lambda e, dc=dc: e.transpose(psTf[:], sigall[:, dc * 128:(dc + 1) * 128], eyef[:]),
                    [tcs[0], l_eye] + ([trp] if trp else []))
        ev = p.emit("vector", lambda e, dc=dc: e.tensor_copy(scat[:, dc, 0:72], psTf[:, 0:72]), [tr])
        trp = ev
    tdk = p.emit("vector", lambda e: e.tensor_sub(dks[:], knots[:, 1:8], knots[:, 0:7]), [l_kn])
    ti1 = p.emit("vector", lambda e: e.tensor_scalar(invt[:], invt[:], 1e-6, None, ALU.add), [l_tp])
    ti2 = p.emit("vector", lambda e: e.reciprocal(invt[:], invt[:]), [ti1])
    tc1 = p.emit("vector", lambda e: e.tensor_copy(cw1b[:].rearrange("p a b -> p (a b)"),
                                                   cw1f[:].rearrange("p a b -> p (a b)")), [l_cw1])
    tc2 = p.emit("vector", lambda e: e.tensor_copy(cw2b[:].rearrange("p a b -> p (a b)"),
                                                   cw2f[:].rearrange("p a b -> p (a b)")), [l_cw2])
    bar0 = p.barrier()

    slot_free = [None, None]
    ost_free = [None, None]
    st_toks = []
    prev = {}
    for tt in range(NT):
        s = tt % 2
        ts = slice(tt * 128, (tt + 1) * 128)
        ld = p.dma("sync", xbuf[:, s, :], x_in[ts, :], [slot_free[s]] + (bar0 if tt == 0 else []))
        lt = emit_ln(p, xbuf[:, s, :], xn[:], r1[:], r2[:], sqs[:], [ld] + [prev.get("xn_used")])
        trp = prev.get("trp")
        ev_toks = []
        for dc in range(ND):
            tr = p.emit("tensor", lambda e, dc=dc: e.transpose(psTf[:], xn[:, dc * 128:(dc + 1) * 128], eyef[:]),
                        [lt] + ([trp] if trp else []))
            ev = p.emit("vector", lambda e, dc=dc: e.tensor_copy(xnTf[:, dc, :], psTf[:]), [tr, prev.get("mmS_done"), prev.get("mmH_done")])
            cb = p.emit("scalar", lambda e, dc=dc: e.copy(xnTb[:, dc, :], xnTf[:, dc, :]), [ev])
            trp = ev
            ev_toks.append((ev, cb))
        prev["trp"] = trp
        mm = None
        for dc in range(ND):
            mm = p.emit("tensor", lambda e, dc=dc: e.matmul(psS[:, 0:72], xnTf[:, dc, :], scat[:, dc, :],
                                                            start=(dc == 0), stop=(dc == ND - 1)),
                        ([ev_toks[dc][0]] if dc else [ev_toks[0][0], prev.get("psS_used")]))
            if dc < ND - 1:
                mm = p.emit("tensor", lambda e: e.engine_nop(), [mm]) if False else mm
                pass
        tf1 = p.emit("vector", lambda e: e.tensor_scalar(fsb[:], psS[:, 0:72], invt[:], None, ALU.mult), [mm, ti2])
        prev["psS_used"] = tf1
        tf2 = p.emit("scalar", lambda e: e.activation(fsb[:], fsb[:], AF.Sigmoid), [tf1])
        tf3 = p.emit("vector", lambda e: e.tensor_scalar(fsb[:], fsb[:], 7.0, None, ALU.mult), [tf2])
        tg_ = p.emit("vector", lambda e: e.tensor_scalar(gsb[:], fsb[:], 0.0, knots[:, 0:1], ALU.mult, ALU.add), [tf3, l_kn])
        for j in range(7):
            ta = p.emit("vector", lambda e, j=j: e.tensor_scalar(csb[:], fsb[:], float(j), 0.0, ALU.subtract, ALU.max), [tf3, tg_])
            tb = p.emit("vector", lambda e: e.tensor_scalar(csb[:], csb[:], 1.0, None, ALU.min), [ta])
            tg_ = p.emit("vector", lambda e, j=j: e.scalar_tensor_tensor(gsb[:], csb[:], dks[:, j:j + 1], gsb[:],
                                                                         ALU.mult, ALU.add), [tb, tdk, tg_])
        tm = p.emit("vector", lambda e: e.reduce_max(r1[:], gsb[:, 64:72], axis=AX.X), [tg_])
        te8 = p.emit("vector", lambda e: e.tensor_scalar(e8[:], gsb[:, 64:72], r1[:], None, ALU.is_ge), [tm])
        tv8 = p.emit("vector", lambda e: e.tensor_mul(e8[:], e8[:], w8t[:]), [te8, l_w8])
        tr8 = p.emit("vector", lambda e: e.reduce_max(r1[:], e8[:], axis=AX.X), [tv8])
        to8 = p.emit("vector", lambda e: e.tensor_scalar(oh8[:], e8[:], r1[:], None, ALU.is_equal), [tr8])
        tr_l = []
        for l in range(8):
            tr_l.append(p.emit("vector", lambda e, l=l: e.tensor_copy(oh64[:, l:64:8].rearrange("p (a b) -> p a b", b=1) if False else oh64[:, l:64:8], oh8[:]), [to8]))
        tmm = p.emit("vector", lambda e: e.tensor_scalar(tm64[:], oh64[:], 1.0, 1e30, ALU.subtract, ALU.mult), tr_l)
        tms = p.emit("vector", lambda e: e.tensor_add(msb[:], tm64[:], gsb[:, 0:64]), [tmm])
        tm2 = p.emit("vector", lambda e: e.reduce_max(r1[:], msb[:], axis=AX.X), [tms])
        te64 = p.emit("vector", lambda e: e.tensor_scalar(e64[:], msb[:], r1[:], None, ALU.is_ge), [tm2])
        tv64 = p.emit("vector", lambda e: e.tensor_mul(e64[:], e64[:], w64t[:]), [te64, l_w64])
        tr64 = p.emit("vector", lambda e: e.reduce_max(r1[:], e64[:], axis=AX.X), [tv64])
        toh = p.emit("vector", lambda e: e.tensor_scalar(ohsb[:], e64[:], r1[:], None, ALU.is_equal), [tr64])
        t_tr = p.emit("tensor", lambda e: e.transpose(psTf[0:64, 0:128], ohsb[:], eyef[:]),
                      [toh, prev.get("trp")])
        prev["trp"] = t_tr
        t_ohT = p.emit("vector", lambda e: e.tensor_copy(ohT[:], psTf[0:64, :]), [t_tr])
        mmh = None
        for dc in range(ND):
            mmh = p.emit("tensor", lambda e, dc=dc: e.matmul(psH[:], xnTb[:, dc, :], cw1b[:, dc, :],
                                                             start=(dc == 0), stop=(dc == ND - 1)),
                         ([ev_toks[dc][1]] if dc else [ev_toks[0][1], tc1, prev.get("psH_used")]))
        th1 = p.emit("vector", lambda e: e.tensor_add(hgel[:], psH[:], cb1t[:]), [mmh, l_cb1])
        prev["psH_used"] = th1
        prev["mmH_done"] = mmh
        th2 = p.emit("scalar", lambda e: e.activation(hgel[:], hgel[:], AF.Gelu), [th1])
        th3 = p.emit("vector", lambda e: e.tensor_copy(hgb[:], hgel[:]), [th2])
        trh = None
        for kc in range(2):
            trh = p.emit("tensor", lambda e, kc=kc: e.transpose(psTb[:], hgb[:, kc * 128:(kc + 1) * 128], eyeb[:]),
                         [th3, l_eyb] + ([trh] if trh else [prev.get("psTb_used")]))
            evh = p.emit("vector", lambda e, kc=kc: e.tensor_copy(hTb[:, kc, :], psTb[:]), [trh])
            trh = evh
        prev["psTb_used"] = trh
        mmc = None
        for kc in range(2):
            mmc = p.emit("tensor", lambda e, kc=kc: e.matmul(psC2[:, 0:2], hTb[:, kc, :], cw2b[:, kc, :],
                                                             start=(kc == 0), stop=(kc == 1)),
                         [trh, tc2] if kc == 0 else [mmc])
        tcp1 = p.emit("vector", lambda e: e.tensor_add(comp[:], psC2[:, 0:2], cb2t[:]), [mmc, l_cb2])
        tcp = p.emit("scalar", lambda e: e.activation(comp[:], comp[:], AF.Tanh), [tcp1])
        mw = p.emit("tensor", lambda e: e.matmul(psW[:, 0:65], ohT[:], spl[:], start=True, stop=True),
                    [t_ohT, l_spl, prev.get("psW_used")])
        tws = p.emit("scalar", lambda e: e.copy(Wsel[:], psW[:, 0:65]), [mw])
        prev["psW_used"] = tws
        md1 = p.emit("tensor", lambda e: e.matmul(psD1[:], ohT[:], dirs[:, 0:512], start=True, stop=True),
                     [t_ohT, l_dir, prev.get("psD1_used")])
        md2 = p.emit("tensor", lambda e: e.matmul(psD2[:], ohT[:], dirs[:, 512:1024], start=True, stop=True),
                     [t_ohT, l_dir, prev.get("psD2_used")])
        ts1 = p.emit("vector", lambda e: e.scalar_tensor_tensor(tsp[:], Wsel[:, 0:16], comp[:, 0:1], Wsel[:, 32:48],
                                                                ALU.mult, ALU.add), [tws, tcp])
        ts2 = p.emit("vector", lambda e: e.scalar_tensor_tensor(tsp[:], Wsel[:, 16:32], comp[:, 1:2], tsp[:],
                                                                ALU.mult, ALU.add), [ts1])
        ts3 = p.emit("vector", lambda e: e.tensor_scalar(hhs[:], tsp[:], 0.0, None, ALU.max), [ts2])
        ts4 = p.emit("vector", lambda e: e.tensor_mul(hhs[:], hhs[:], Wsel[:, 48:64]), [ts3])
        ts5 = p.emit("vector", lambda e: e.reduce_sum(magv[:], hhs[:], axis=AX.X), [ts4])
        ts6 = p.emit("vector", lambda e: e.tensor_add(magv[:], magv[:], Wsel[:, 64:65]), [ts5])
        ts7 = p.emit("vector", lambda e: e.tensor_mul(magv[:], magv[:], osc[:]), [ts6, l_osc])
        ty1 = p.emit("vector", lambda e: e.scalar_tensor_tensor(ysb[:, 0:512], psD1[:], magv[:], xn[:, 0:512],
                                                                ALU.mult, ALU.add), [md1, ts7, lt])
        ty2 = p.emit("vector", lambda e: e.scalar_tensor_tensor(ysb[:, 512:1024], psD2[:], magv[:], xn[:, 512:1024],
                                                                ALU.mult, ALU.add), [md2, ts7, lt])
        prev["psD1_used"] = ty1
        prev["psD2_used"] = ty2
        prev["xn_used"] = ty2
        lz = emit_ln(p, ysb[:], zsb[:], r1[:], r2[:], sqs[:], [ty1, ty2], gamma=gam[:], beta=bet[:])
        lp = emit_ln(p, zsb[:], ysb[:], r1[:], r2[:], sqs[:], [lz])
        ta = p.emit("vector", lambda e, s=s: e.tensor_add(outbuf[:, s, :], ysb[:], xbuf[:, s, :]),
                    [lp, ost_free[s]])
        slot_free[s] = ta
        st = p.dma("gpsimd", xout[ts, :], outbuf[:, s, :], [ta])
        ost_free[s] = st
        st_toks.append(st)
        prev["mmS_done"] = mm
    p.emit("gpsimd", lambda e: e.engine_nop(), st_toks)
    if ctx is None:
        p.build(nc, stack)
        stack.close()
        return nc
    stack.close()
    return None


def build_block_c(ctx=None):
    if ctx is None:
        nc = bass.Bass("TRN2", target_bir_lowering=False)
        p = Prog()
        pfx = ""
        x_io = None
    else:
        nc, p, pfx, x_io = ctx
    if x_io is None:
        x_in = nc.dram_tensor("x_in", [T, E], F32, kind="ExternalInput").ap()
        xout = nc.dram_tensor("xout", [T, E], F32, kind="ExternalOutput").ap()
    else:
        x_in, xout = x_io
    whd = nc.dram_tensor(pfx + "whd", [NH, 128, ND, 192], F32, kind="ExternalInput").ap()
    bqk_d = nc.dram_tensor(pfx + "bqk", [128, NH], F32, kind="ExternalInput").ap()
    bv_d = nc.dram_tensor(pfx + "bv", [64, NH], F32, kind="ExternalInput").ap()
    wout_d = nc.dram_tensor(pfx + "wout", [E, E], F32, kind="ExternalInput").ap()
    bout_d = nc.dram_tensor(pfx + "bout", [128, E], F32, kind="ExternalInput").ap()
    cos_d = nc.dram_tensor(pfx + "cosT", [64, T], F32, kind="ExternalInput").ap()
    sin_d = nc.dram_tensor(pfx + "sinT", [64, T], F32, kind="ExternalInput").ap()
    perm_d = nc.dram_tensor(pfx + "perm", [64, 64], F32, kind="ExternalInput").ap()
    eyeb_d = nc.dram_tensor(pfx + "eyeb", [128, 128], BF16, kind="ExternalInput").ap()

    stack = contextlib.ExitStack()
    sb = lambda name, shape, dt: stack.enter_context(nc.sbuf_tensor(pfx + name, shape, dt))
    _psn = [0]
    def ps(shape, dt):
        _psn[0] += 1
        return stack.enter_context(nc.psum_tensor(f"{pfx}ps{_psn[0]}", shape, dt))

    yTd = nc.dram_tensor(pfx + "yTd", [ND, 128, T], BF16, kind="Internal").ap()
    xnT = sb("xnT", [128, ND, T], BF16)
    seyeb = sb("seyeb", [128, 128], BF16)
    cosT = sb("cosT_s", [64, T], F32)
    sinT = sb("sinT_s", [64, T], F32)
    permf = sb("permf", [64, 64], F32)
    bqk = sb("bqk_s", [128, NH], F32)
    bv = sb("bv_s", [64, NH], F32)
    xbuf = sb("xbuf", [128, 2, E], F32)
    xn = sb("xn", [128, E], F32)
    xnb = sb("xnb", [128, E], BF16)
    r1 = sb("r1", [128, 1], F32)
    r2 = sb("r2", [128, 1], F32)
    sqs = sb("sqs", [128, E], F32)
    wpbuf = sb("wpbuf", [128, E], F32)
    wpTb = sb("wpTb", [128, ND, E], BF16)
    attn_sb = sb("attn_sb", [128, E], F32)
    bout_t = sb("bout_t", [128, E], F32)
    outbuf = sb("outbuf", [128, 2, E], F32)
    ytt = sb("ytt", [128, ND, 128], BF16)
    stack2 = contextlib.ExitStack()
    sb2 = lambda name, shape, dt: stack2.enter_context(nc.sbuf_tensor(pfx + "2" + name, shape, dt))
    vo = sb2("vo", [128, NT, 64], BF16)
    vtb = sb2("vtb", [64, 512], BF16)
    phbq = sb2("phbq", [64, T], BF16)
    phbk = sb2("phbk", [64, T], BF16)
    rotq = sb2("rotq", [64, T], BF16)
    rotk = sb2("rotk", [64, T], BF16)
    qkb = xn[:, 0:512]
    phtq = xbuf[0:64, 0, 0:512]
    phtk = xbuf[0:64, 0, 512:1024]
    mn = sqs[:, 0:512]
    ex = sqs[:, 512:1024]
    tmpr = xbuf[0:64, 1, 0:512]
    rotb = xbuf[0:64, 1, 512:1024]
    whbuf = sb2("whbuf", [128, ND, 192], F32)
    whb = sb2("whb", [128, ND, 192], BF16)
    Wbuf = sb2("Wbuf", [128, 2, 512], BF16)
    numf = sb2("numf", [128, 512], F32)
    rdsb = sb2("rdsb", [128, 512], F32)
    ysb = sb2("ysb", [64, 512], BF16)

    psT = ps([128, 128], BF16)
    psQK = ps([128, 512], F32)
    psSw = ps([64, 512], F32)
    psN = ps([128, 512], F32)
    psD2 = ps([128, 512], F32)
    psO = ps([64, 512], F32)
    psP1 = ps([128, 512], F32)
    psP2 = ps([128, 512], F32)

    l_eye = p.dma("sync", seyeb[:], eyeb_d)
    l_cos = p.dma("sync", cosT[:], cos_d)
    l_sin = p.dma("sync", sinT[:], sin_d)
    l_perm = p.dma("sync", permf[:], perm_d)
    l_bqk = p.dma("sync", bqk[:], bqk_d)
    l_bv = p.dma("sync", bv[:], bv_d)

    slot_free = [None, None]
    prev_ev = []
    for tt in range(NT):
        s = tt % 2
        ld = p.dma("sync", xbuf[:, s, :], x_in[tt * 128:(tt + 1) * 128, :], [slot_free[s]])
        lt, ct, evs = emit_xnt_tile(
            p, tt, xbuf[:, s, :], xn[:], xnb[:],
            lambda dc, tt=tt: xnT[:, dc, tt * 128:(tt + 1) * 128],
            r1[:], r2[:], sqs[:], psT, seyeb[:], [ld, l_eye], prev_ev)
        prev_ev = evs
        slot_free[s] = lt

    bar1 = p.barrier()

    w_free = None
    psqk_free = None
    pssw_free = None
    vtb_free = None
    psN_free = [None, None]
    pr_ysb_free = [None]
    prev_w = None
    psD_free = [None, None]
    psO_free = None
    wslot_free = [None, None]
    nslot_free = [None, None]
    rslot_free = [None, None]
    for h in range(NH):
        hp0 = (h % 2) * 64
        hdc = h // 2
        ldw = p.dma("sync", whbuf[:], whd[h], [w_free] + (bar1 if h == 0 else []))
        cw = p.emit("vector", lambda e: e.tensor_copy(whb[:].rearrange("p a b -> p (a b)"),
                                                      whbuf[:].rearrange("p a b -> p (a b)")), [ldw])
        w_free = cw
        phi_toks = []
        for tg in range(8):
            gs = slice(tg * 512, (tg + 1) * 512)
            mm = None
            for dc in range(ND):
                mm = p.emit("tensor",
                            lambda e, dc=dc, gs=gs: e.matmul(psQK[:], whb[:, dc, 0:128], xnT[:, dc, gs],
                                                             start=(dc == 0), stop=(dc == ND - 1)),
                            ([cw, psqk_free] if dc == 0 else [mm]))
            tb_ = p.emit("vector", lambda e, h=h: e.tensor_scalar(qkb, psQK[:], bqk[:, h:h + 1], None, ALU.add),
                         [mm, l_bqk])
            psqk_free = tb_
            for b0, pht, phdst, rotdst in ((0, phtq, phbq, rotq), (64, phtk, phbk, rotk)):
                if b0 == 0:
                    mnh = sqs[0:64, 0:512]
                    exh = sqs[0:64, 512:1024]
                else:
                    mnh = attn_sb[0:64, 0:512]
                    exh = attn_sb[0:64, 512:1024]
                t_mn = p.emit("vector", lambda e, b0=b0, mnh=mnh: e.tensor_scalar(mnh, qkb[b0:b0 + 64, :], 0.0, None, ALU.min), [tb_])
                t_ex = p.emit("scalar", lambda e, mnh=mnh, exh=exh: e.activation(exh, mnh, AF.Exp), [t_mn])
                t_rx = p.emit("gpsimd", lambda e, b0=b0, pht=pht: e.tensor_scalar(pht, qkb[b0:b0 + 64, :], 0.0, None, ALU.max), [tb_])
                t_phi = p.emit("vector", lambda e, pht=pht, exh=exh: e.tensor_add(pht, exh, pht), [t_ex, t_rx])
                t_pb = p.emit("scalar", lambda e, gs=gs, phdst=phdst, pht=pht: e.copy(phdst[:, gs], pht), [t_phi])
                msw = p.emit("tensor",
                             lambda e, pht=pht: e.matmul(psSw[:], permf[:], pht,
                                                         start=True, stop=True, skip_group_check=True),
                             [t_phi, l_perm, pssw_free])
                t_t = p.emit("vector", lambda e, gs=gs: e.tensor_mul(tmpr, psSw[:], sinT[:, gs]), [msw, l_sin])
                pssw_free = t_t
                t_r = p.emit("gpsimd", lambda e, pht=pht, gs=gs: e.tensor_mul(rotb, pht, cosT[:, gs]), [t_phi, l_cos, t_t])
                t_a = p.emit("vector", lambda e, gs=gs, rotdst=rotdst: e.tensor_add(rotdst[:, gs], rotb, tmpr), [t_t, t_r])
                pssw_free = t_a
                phi_toks.append(Tok(t_pb.eng, t_pb.val))
                phi_toks.append(t_a)
        for tg in range(8):
            gs = slice(tg * 512, (tg + 1) * 512)
            mm = None
            for dc in range(ND):
                mm = p.emit("tensor",
                            lambda e, dc=dc, gs=gs: e.matmul(psQK[0:64, :], whb[:, dc, 128:192], xnT[:, dc, gs],
                                                             start=(dc == 0), stop=(dc == ND - 1)),
                            ([psqk_free] if dc == 0 else [mm]))
            cv = p.emit("vector", lambda e, h=h: e.tensor_scalar(vtb[:], psQK[0:64, :], bv[:, h:h + 1], None, ALU.add),
                        [mm, l_bv, vtb_free])
            psqk_free = cv
            trv = None
            for k in range(4):
                jc = tg * 4 + k
                trv = p.emit("tensor",
                             lambda e, k=k: e.transpose(psT[:, 0:64], vtb[0:64, k * 128:(k + 1) * 128], seyeb[0:64, 0:64]),
                             [cv] + ([trv] if trv else [prev_ev[-1] if prev_ev else None]))
                evv = p.emit("vector", lambda e, jc=jc: e.tensor_copy(vo[:, jc, :], psT[:, 0:64]), [trv])
                trv = evv
            vtb_free = trv
        for g in range(8):
            gs = slice(g * 512, (g + 1) * 512)
            mmo_last = None
            for jc in range(NT):
                js = slice(jc * 128, (jc + 1) * 128)
                sl = jc % 2
                mmN = p.emit("tensor",
                             lambda e, js=js, gs=gs: e.matmul(psN[:], rotk[:, js], rotq[:, gs],
                                                              start=True, stop=True, skip_group_check=True),
                             [psN_free[sl]] + (phi_toks if jc == 0 and g == 0 else []))
                mmD = p.emit("tensor",
                             lambda e, js=js, gs=gs: e.matmul(psD2[:], phbk[:, js], phbq[:, gs],
                                                              start=True, stop=True, skip_group_check=True),
                             [psD_free[sl]])
                t_rd = p.emit("vector", lambda e: e.reciprocal(rdsb[:], psD2[:]), [mmD, prev_w])
                psD_free[sl] = t_rd
                t_nv = p.emit("scalar", lambda e: e.copy(numf[:], psN[:]), [mmN, prev_w])
                psN_free[sl] = t_nv
                t_w = p.emit("gpsimd", lambda e, sl=sl: e.tensor_mul(Wbuf[:, sl, :], numf[:], rdsb[:]),
                             [t_rd, t_nv, wslot_free[sl]])
                prev_w = t_w
                mmo_last = p.emit("tensor",
                                  lambda e, jc=jc, sl=sl: e.matmul(psO[:], vo[:, jc, :], Wbuf[:, sl, :],
                                                                   start=(jc == 0), stop=(jc == NT - 1),
                                                                   skip_group_check=True),
                                  [t_w] + ([psO_free] if jc == 0 else []))
                wslot_free[sl] = mmo_last
            ty = p.emit("scalar",
                        lambda e: e.copy(ysb[:], psO[:]),
                        [mmo_last, pr_ysb_free[0]])
            psO_free = ty
            tyd = p.dma("gpsimd", yTd[hdc, hp0:hp0 + 64, gs], ysb[:], [ty])
            pr_ysb_free[0] = tyd

    bar2 = p.barrier()
    stack2.close()

    l_bo = p.dma("sync", bout_t[:], bout_d, bar2)
    cwp = None
    for dc in range(ND):
        ldp = p.dma("sync", wpbuf[:], wout_d[dc * 128:(dc + 1) * 128, :], bar2 + [cwp])
        cwp = p.emit("vector", lambda e, dc=dc: e.tensor_copy(wpTb[:, dc, :], wpbuf[:]), [ldp])
    slot_free = [None, None]
    ost_free = [None, None]
    ev_free = None
    st_toks = []
    yld_free = None
    for tt in range(NT):
        s = tt % 2
        ts = slice(tt * 128, (tt + 1) * 128)
        yld = [p.dma("sync", ytt[:, dc, :], yTd[dc, :, ts], [yld_free]) for dc in range(ND)]
        mms = []
        for oh, psP in ((0, psP1), (1, psP2)):
            mm = None
            for dc in range(ND):
                mm = p.emit("tensor",
                            lambda e, dc=dc, oh=oh, psP=psP: e.matmul(
                                psP[:], ytt[:, dc, :], wpTb[:, dc, oh * 512:(oh + 1) * 512],
                                start=(dc == 0), stop=(dc == ND - 1)),
                            ([cwp, ev_free] + yld if dc == 0 else [mm]))
            mms.append(mm)
        yld_free = mms[-1]
        e1 = p.emit("vector", lambda e: e.tensor_add(attn_sb[:, 0:512], psP1[:], bout_t[:, 0:512]), [mms[0], l_bo])
        e2 = p.emit("vector", lambda e: e.tensor_add(attn_sb[:, 512:1024], psP2[:], bout_t[:, 512:1024]), [mms[1], l_bo])
        ev_free = e2
        lt = emit_ln(p, attn_sb[:], sqs[:], r1[:], r2[:], xn[:], [e1, e2])
        ldx = p.dma("sync", xbuf[:, s, :], x_in[ts, :], [slot_free[s]])
        ta = p.emit("vector", lambda e, s=s: e.tensor_add(outbuf[:, s, :], sqs[:], xbuf[:, s, :]), [lt, ldx, ost_free[s]])
        slot_free[s] = ta
        st = p.dma("gpsimd", xout[ts, :], outbuf[:, s, :], [ta])
        ost_free[s] = st
        st_toks.append(st)
    p.emit("gpsimd", lambda e: e.engine_nop(), st_toks)
    if ctx is None:
        p.build(nc, stack)
        stack.close()
        return nc
    stack.close()
    return None


def build_block_d(ctx=None):
    if ctx is None:
        nc = bass.Bass("TRN2", target_bir_lowering=False)
        p = Prog()
        pfx = ""
        x_io = None
    else:
        nc, p, pfx, x_io = ctx
    if x_io is None:
        x_in = nc.dram_tensor("x_in", [T, E], F32, kind="ExternalInput").ap()
        xout = nc.dram_tensor("xout", [T, E], F32, kind="ExternalOutput").ap()
    else:
        x_in, xout = x_io
    W1b_d = nc.dram_tensor(pfx + "W1b", [32, 128, 8, 128], F32, kind="ExternalInput").ap()
    W2b_d = nc.dram_tensor(pfx + "W2b", [32, 128, 8, 128], F32, kind="ExternalInput").ap()
    H1b_d = nc.dram_tensor(pfx + "H1b", [32, 128, 32, 128], F32, kind="ExternalInput").ap()
    H2b_d = nc.dram_tensor(pfx + "H2b", [32, 128, 32, 128], F32, kind="ExternalInput").ap()
    W3b_d = nc.dram_tensor(pfx + "W3b", [8, 128, 32, 128], F32, kind="ExternalInput").ap()
    s1_d = nc.dram_tensor(pfx + "s1t", [128, 32], F32, kind="ExternalInput").ap()
    s2_d = nc.dram_tensor(pfx + "s2t", [128, 32], F32, kind="ExternalInput").ap()
    sh1_d = nc.dram_tensor(pfx + "sh1t", [128, 32], F32, kind="ExternalInput").ap()
    sh2_d = nc.dram_tensor(pfx + "sh2t", [128, 32], F32, kind="ExternalInput").ap()
    s3_d = nc.dram_tensor(pfx + "s3t", [128, 8], F32, kind="ExternalInput").ap()
    eyeb_d = nc.dram_tensor(pfx + "eyeb", [128, 128], BF16, kind="ExternalInput").ap()

    stack = contextlib.ExitStack()
    sb = lambda name, shape, dt: stack.enter_context(nc.sbuf_tensor(pfx + name, shape, dt))
    _psn = [0]
    def ps(shape, dt):
        _psn[0] += 1
        return stack.enter_context(nc.psum_tensor(f"{pfx}ps{_psn[0]}", shape, dt))

    seyeb = sb("seyeb", [128, 128], BF16)
    s1t = sb("s1t_s", [128, 32], F32)
    s2t = sb("s2t_s", [128, 32], F32)
    sh1t = sb("sh1t_s", [128, 32], F32)
    sh2t = sb("sh2t_s", [128, 32], F32)
    s3t = sb("s3t_s", [128, 8], F32)
    xnTb = sb("xnTb", [128, ND, 512], BF16)
    bufA = sb("bufA", [128, 32, 512], BF16)
    bufB = sb("bufB", [128, 32, 512], BF16)
    hf = [sb("hf0", [128, 32, 128], F32), sb("hf1", [128, 32, 128], F32)]
    hc = sb("hc", [128, 32, 128], BF16)
    gf = [sb("gf0", [128, 8, 128], F32), sb("gf1", [128, 8, 128], F32)]
    gc = sb("gc", [128, 8, 128], BF16)
    asb = sb("asb", [128, 512], F32)
    bsb = sb("bsb", [128, 512], F32)
    csb2 = sb("csb2", [128, 512], F32)
    stg = sb("stg", [128, 512], BF16)
    tokf = sb("tokf", [128, 4, E], F32)
    xbuf = sb("xbuf", [128, 2, E], F32)
    xn = sb("xn", [128, E], F32)
    xnb = sb("xnb", [128, E], BF16)
    r1 = sb("r1", [128, 1], F32)
    r2 = sb("r2", [128, 1], F32)
    sqs = sb("sqs", [128, E], F32)
    outbuf = sb("outbuf", [128, 2, E], F32)

    psT = ps([128, 128], BF16)
    psG1 = ps([128, 512], F32)
    psG2 = ps([128, 512], F32)
    psH = ps([128, 512], F32)
    psW3 = ps([128, 512], F32)

    l_eye = p.dma("sync", seyeb[:], eyeb_d)
    l_s = [p.dma("sync", t[:], d) for t, d in
           ((s1t, s1_d), (s2t, s2_d), (sh1t, sh1_d), (sh2t, sh2_d), (s3t, s3_d))]

    st_toks = []
    slot_free = [None, None]
    ost_free = [None, None]
    pr = {"ev": [], "last_w2mm": None, "gsl_free": [None, None], "gcast_free": None,
          "psg_free": [None, None], "hsl_free": [None, None], "hcast_free": None,
          "psh_free": None, "asb_used": None, "csb_used": None, "bufA_guard": None,
          "bufB_guard": None, "stg_free": None, "psw3_free": None, "tokf_free": None,
          "w3sl_free": [None, None]}
    for tb in range(8):
        prev_ev = pr["ev"]
        xnt_last = None
        for k in range(4):
            tt = tb * 4 + k
            s = tt % 2
            ld = p.dma("sync", xbuf[:, s, :], x_in[tt * 128:(tt + 1) * 128, :], [slot_free[s]])
            lt, ct, evs = emit_xnt_tile(
                p, tt, xbuf[:, s, :], xn[:], xnb[:],
                lambda dc, k=k: xnTb[:, dc, k * 128:(k + 1) * 128],
                r1[:], r2[:], sqs[:], psT, seyeb[:],
                [ld, l_eye, pr["last_w2mm"]], prev_ev)
            prev_ev = evs
            slot_free[s] = lt
            xnt_last = evs[-1]
        pr["ev"] = prev_ev
        ge_last = None
        for oc in range(32):
            sl = oc % 2
            ld1 = p.dma("sync", gf[sl][:].rearrange("p a b -> p (a b)"),
                        W1b_d[oc].rearrange("p a b -> p (a b)"), [pr["gsl_free"][sl]])
            c1 = p.emit("vector", lambda e, sl=sl: e.tensor_copy(gc[:].rearrange("p a b -> p (a b)"),
                                                                 gf[sl][:].rearrange("p a b -> p (a b)")),
                        [ld1, pr["gcast_free"]])
            mm1 = None
            for dc in range(ND):
                mm1 = p.emit("tensor", lambda e, dc=dc: e.matmul(psG1[:], gc[:, dc, :], xnTb[:, dc, :],
                                                                 start=(dc == 0), stop=(dc == ND - 1)),
                             ([c1, pr["psg_free"][0], xnt_last] if dc == 0 else [mm1]))
            ld2 = p.dma("scalar", gf[sl][:].rearrange("p a b -> p (a b)"),
                        W2b_d[oc].rearrange("p a b -> p (a b)"), [c1])
            c2 = p.emit("vector", lambda e, sl=sl: e.tensor_copy(gc[:].rearrange("p a b -> p (a b)"),
                                                                 gf[sl][:].rearrange("p a b -> p (a b)")),
                        [ld2, mm1])
            pr["gsl_free"][sl] = c2
            mm2 = None
            for dc in range(ND):
                mm2 = p.emit("tensor", lambda e, dc=dc: e.matmul(psG2[:], gc[:, dc, :], xnTb[:, dc, :],
                                                                 start=(dc == 0), stop=(dc == ND - 1)),
                             ([c2, pr["psg_free"][1]] if dc == 0 else [mm2]))
            pr["gcast_free"] = mm2
            pr["last_w2mm"] = mm2
            ta_ = p.emit("vector", lambda e, oc=oc: e.tensor_scalar(asb[:], psG1[:], s1t[:, oc:oc + 1], None, ALU.mult),
                         [mm1, l_s[0], pr["asb_used"]])
            tb_ = p.emit("scalar", lambda e: e.activation(bsb[:], asb[:], AF.Silu), [ta_])
            tc_ = p.emit("vector", lambda e, oc=oc: e.tensor_scalar(csb2[:], psG2[:], s2t[:, oc:oc + 1], None, ALU.mult),
                         [mm2, l_s[1], pr["csb_used"]])
            th_ = p.emit("gpsimd", lambda e, oc=oc: e.tensor_mul(bufA[:, oc, :], bsb[:], csb2[:]),
                         [tb_, tc_, pr["bufA_guard"]])
            pr["psg_free"] = [ta_, tc_]
            pr["asb_used"] = th_
            pr["csb_used"] = th_
            ge_last = th_
        in_last = ge_last
        for phase in range(2):
            wd = H1b_d if phase == 0 else H2b_d
            st_ = sh1t if phase == 0 else sh2t
            hin = bufA if phase == 0 else bufB
            hout = bufB if phase == 0 else bufA
            guard = pr["bufB_guard"] if phase == 0 else pr["last_h1mm"]
            h_last = None
            last_mm = None
            for oc in range(32):
                sl = oc % 2
                ldh = p.dma("sync", hf[sl][:].rearrange("p a b -> p (a b)"),
                            wd[oc].rearrange("p a b -> p (a b)"), [pr["hsl_free"][sl]])
                ch = p.emit("vector", lambda e, sl=sl: e.tensor_copy(hc[:].rearrange("p a b -> p (a b)"),
                                                                     hf[sl][:].rearrange("p a b -> p (a b)")),
                            [ldh, pr["hcast_free"]])
                pr["hsl_free"][sl] = ch
                mm = None
                for dc in range(32):
                    mm = p.emit("tensor", lambda e, dc=dc, hin=hin: e.matmul(psH[:], hc[:, dc, :], hin[:, dc, :],
                                                                             start=(dc == 0), stop=(dc == 31)),
                                ([ch, pr["psh_free"], in_last] if dc == 0 else [mm]))
                pr["hcast_free"] = mm
                last_mm = mm
                ta_ = p.emit("vector", lambda e, oc=oc, st_=st_: e.tensor_scalar(asb[:], psH[:], st_[:, oc:oc + 1],
                                                                                 None, ALU.mult),
                             [mm, pr["asb_used"], l_s[2]])
                pr["psh_free"] = ta_
                tg_ = p.emit("scalar", lambda e, oc=oc, hout=hout: e.activation(hout[:, oc, :], asb[:], AF.Gelu),
                             [ta_] + ([guard] if oc == 0 else []))
                pr["asb_used"] = tg_
                h_last = tg_
            in_last = h_last
            if phase == 0:
                pr["last_h1mm"] = last_mm
        trp = pr["ev"][-1] if pr["ev"] else None
        last_w3mm = None
        for oc3 in range(8):
            sl = oc3 % 2
            ld3 = p.dma("sync", hf[sl][:].rearrange("p a b -> p (a b)"),
                        W3b_d[oc3].rearrange("p a b -> p (a b)"), [pr["hsl_free"][sl]])
            c3 = p.emit("vector", lambda e, sl=sl: e.tensor_copy(hc[:].rearrange("p a b -> p (a b)"),
                                                                 hf[sl][:].rearrange("p a b -> p (a b)")),
                        [ld3, pr["hcast_free"]])
            pr["hsl_free"][sl] = c3
            mm = None
            for dc in range(32):
                mm = p.emit("tensor", lambda e, dc=dc: e.matmul(psW3[:], hc[:, dc, :], bufA[:, dc, :],
                                                                start=(dc == 0), stop=(dc == 31)),
                            ([c3, pr["psw3_free"], in_last] if dc == 0 else [mm]))
            pr["hcast_free"] = mm
            last_w3mm = mm
            tsc = p.emit("vector", lambda e, oc3=oc3: e.tensor_scalar(stg[:], psW3[:], s3t[:, oc3:oc3 + 1],
                                                                      None, ALU.mult),
                         [mm, l_s[4], pr["stg_free"]])
            pr["psw3_free"] = tsc
            trv = None
            for k in range(4):
                trv = p.emit("tensor",
                             lambda e, k=k: e.transpose(psT[:, 0:128], stg[:, k * 128:(k + 1) * 128], seyeb[:]),
                             [tsc] + ([trv] if trv else [trp]))
                evv = p.emit("scalar",
                             lambda e, k=k, oc3=oc3: e.copy(tokf[:, k, oc3 * 128:(oc3 + 1) * 128], psT[:, 0:128]),
                             [trv, pr["tokf_free"]])
                trv = evv
            trp = trv
            pr["stg_free"] = trv
        pr["bufA_guard"] = last_w3mm
        pr["bufB_guard"] = pr["last_h1mm"]
        pr["bufB_guard"] = pr["hcast_free"] if False else pr["bufB_guard"]
        pr["bufB_guard"] = last_w3mm
        tok_last = None
        for k in range(4):
            tt = tb * 4 + k
            s = tt % 2
            ts = slice(tt * 128, (tt + 1) * 128)
            lt = emit_ln(p, tokf[:, k, :], xn[:], r1[:], r2[:], sqs[:], [pr["stg_free"], slot_free[s]])
            ldx = p.dma("sync", xbuf[:, s, :], x_in[ts, :], [slot_free[s], lt])
            ta = p.emit("vector", lambda e, s=s, k=k: e.tensor_add(outbuf[:, s, :], xn[:], xbuf[:, s, :]),
                        [lt, ldx, ost_free[s]])
            slot_free[s] = ta
            st = p.dma("gpsimd", xout[ts, :], outbuf[:, s, :], [ta])
            ost_free[s] = st
            st_toks.append(st)
            tok_last = ta
        pr["tokf_free"] = tok_last
    p.emit("gpsimd", lambda e: e.engine_nop(), st_toks)
    if ctx is None:
        p.build(nc, stack)
        stack.close()
        return nc
    stack.close()
    return None


_CACHE = {}


def _get(name, builder):
    if name not in _CACHE:
        _CACHE[name] = builder()
    return _CACHE[name]


def _prep(inputs):
    if "prep" in _CACHE:
        return _CACHE["prep"]
    f32 = np.float32
    c = np.ascontiguousarray
    pr = {}
    eyeb = np.eye(128, dtype=BF)
    eyef = np.eye(128, dtype=f32)
    waT = c(inputs["L0_attn_w_attn"].T)
    whead = np.stack([np.concatenate([waT[:, h * 64:(h + 1) * 64],
                                      waT[:, 1024 + h * 64:1024 + (h + 1) * 64],
                                      waT[:, 2048 + h * 64:2048 + (h + 1) * 64]], axis=1)
                      for h in range(16)])
    pr["whead"] = c(whead.reshape(16, 8, 128, 192).transpose(0, 2, 1, 3))
    pr["wpT"] = c(inputs["L0_attn_w_proj"].T)
    dm = np.zeros((4, 128, 512), dtype=f32)
    jl = np.arange(128)[:, None]
    il = np.arange(512)[None, :]
    for pp in range(4):
        dm[pp] = (il >= jl + pp * 128).astype(f32)
    pr["dmask"] = dm.astype(BF)
    pr["eyeb"] = eyeb
    pr["eyef"] = eyef
    pr["knots"] = c(np.tile(inputs["L0_ffn_knot_values"][None, :], (128, 1)))
    pr["temp"] = c(np.tile(inputs["L0_ffn_temperature"][None, :], (128, 1)))
    pr["cw1"] = c(inputs["L0_ffn_compress_w1"])
    pr["cb1"] = c(np.tile(inputs["L0_ffn_compress_b1"][None, :], (128, 1)))
    pr["cw2"] = c(inputs["L0_ffn_compress_w2"].reshape(2, 128, 2))
    pr["cb2"] = c(np.tile(inputs["L0_ffn_compress_b2"][None, :], (128, 1)))
    pr["spl"] = c(np.concatenate([inputs["L0_ffn_spline_w1"][:, 0, :],
                                  inputs["L0_ffn_spline_w1"][:, 1, :],
                                  inputs["L0_ffn_spline_b1"],
                                  inputs["L0_ffn_spline_w2"][:, :, 0],
                                  inputs["L0_ffn_spline_b2"]], axis=1))
    pr["dirs"] = c(inputs["L0_ffn_directions"])
    pr["gam"] = c(np.tile(inputs["L0_ffn_norm_gamma"][None, :], (128, 1)))
    pr["bet"] = c(np.tile(inputs["L0_ffn_norm_beta"][None, :], (128, 1)))
    pr["osc"] = c(np.tile(inputs["L0_ffn_output_scale"][None, :], (128, 1)))
    a8 = np.zeros((64, 8), dtype=f32)
    for cc in range(8):
        a8[cc * 8:(cc + 1) * 8, cc] = 1.0
    pr["a8"] = a8
    pr["w8"] = c(np.tile(np.arange(8, 0, -1, dtype=f32)[None, :], (128, 1)))
    pr["w64"] = c(np.tile(np.arange(64, 0, -1, dtype=f32)[None, :], (128, 1)))
    pr["sraw"] = c(inputs["L0_ffn_signatures_raw"])
    wq = inputs["L1_attn_w_qkv"]
    whd = np.stack([np.concatenate([wq[:, h * 64:(h + 1) * 64],
                                    wq[:, 1024 + h * 64:1024 + (h + 1) * 64],
                                    wq[:, 2048 + h * 64:2048 + (h + 1) * 64]], axis=1)
                    for h in range(16)])
    pr["whd"] = c(whd.reshape(16, 8, 128, 192).transpose(0, 2, 1, 3))
    bq = inputs["L1_attn_b_qkv"]
    bqk = np.zeros((128, 16), dtype=f32)
    bv = np.zeros((64, 16), dtype=f32)
    for h in range(16):
        bqk[0:64, h] = bq[h * 64:(h + 1) * 64]
        bqk[64:128, h] = bq[1024 + h * 64:1024 + (h + 1) * 64]
        bv[:, h] = bq[2048 + h * 64:2048 + (h + 1) * 64]
    pr["bqk"] = bqk
    pr["bv"] = bv
    pr["wout"] = c(inputs["L1_attn_w_out"])
    pr["bout"] = c(np.tile(inputs["L1_attn_b_out"][None, :], (128, 1)))
    half = 32
    freqs = (np.arange(half, dtype=f32) / f32(half)).astype(f32)
    theta = (1.0 / (10000.0 ** freqs)).astype(f32)
    pos = np.arange(T, dtype=f32)
    ang = (pos[:, None] * theta[None, :]).astype(f32)
    cosv = np.cos(ang).astype(f32)
    sinv = np.sin(ang).astype(f32)
    cosT = np.zeros((64, T), dtype=f32)
    sinT = np.zeros((64, T), dtype=f32)
    cosT[0::2, :] = cosv.T
    cosT[1::2, :] = cosv.T
    sinT[0::2, :] = -sinv.T
    sinT[1::2, :] = sinv.T
    pr["cosT"] = cosT
    pr["sinT"] = sinT
    perm = np.zeros((64, 64), dtype=f32)
    for j in range(32):
        perm[2 * j, 2 * j + 1] = 1.0
        perm[2 * j + 1, 2 * j] = 1.0
    pr["perm"] = perm
    def wblocks(w, noc, ndc):
        wT = w.T
        nin = wT.shape[0]
        return c(wT.reshape(ndc, 128, noc, 128).transpose(2, 1, 0, 3))
    pr["W1b"] = wblocks(inputs["L1_ffn_w1_w"], 32, 8)
    pr["W2b"] = wblocks(inputs["L1_ffn_w2_w"], 32, 8)
    pr["H1b"] = wblocks(inputs["L1_ffn_h1_w"], 32, 32)
    pr["H2b"] = wblocks(inputs["L1_ffn_h2_w"], 32, 32)
    pr["W3b"] = wblocks(inputs["L1_ffn_w3_w"], 8, 32)
    pr["s1t"] = c(inputs["L1_ffn_w1_s"].reshape(32, 128).T)
    pr["s2t"] = c(inputs["L1_ffn_w2_s"].reshape(32, 128).T)
    pr["sh1t"] = c(inputs["L1_ffn_h1_s"].reshape(32, 128).T)
    pr["sh2t"] = c(inputs["L1_ffn_h2_s"].reshape(32, 128).T)
    pr["s3t"] = c(inputs["L1_ffn_w3_s"].reshape(8, 128).T)
    _CACHE["prep"] = pr
    return pr


def build_fused():
    nc = bass.Bass("TRN2", target_bir_lowering=False)
    p = Prog()
    x_in = nc.dram_tensor("x_in", [T, E], F32, kind="ExternalInput").ap()
    xout = nc.dram_tensor("xout", [T, E], F32, kind="ExternalOutput").ap()
    x1d = nc.dram_tensor("x1d", [T, E], F32, kind="Internal").ap()
    x2d = nc.dram_tensor("x2d", [T, E], F32, kind="Internal").ap()
    x3d = nc.dram_tensor("x3d", [T, E], F32, kind="Internal").ap()
    stack = contextlib.ExitStack()
    for pfx, fn, io in (("a_", build_block_a, (x_in, x1d)),
                        ("b_", build_block_b, (x1d, x2d)),
                        ("c_", build_block_c, (x2d, x3d)),
                        ("d_", build_block_d, (x3d, xout))):
        fn((nc, p, pfx, io))
        p.default_deps = p.barrier()
    p.build(nc, stack)
    stack.close()
    return nc


def kernel(**inputs):
    pr = _prep(inputs)
    x = np.ascontiguousarray(inputs["L0_attn_x"].astype(np.float32))
    cores = list(range(8))
    nc = _get("fused", build_fused)
    shared = {}
    for k in ("whead", "wpT", "dmask", "eyeb"):
        shared["a_" + k] = pr[k]
    for k in ("sraw", "knots", "temp", "cw1", "cb1", "cw2", "cb2", "spl", "dirs",
              "gam", "bet", "osc", "a8", "w8", "w64", "eyef"):
        shared["b_" + k] = pr[k]
    shared["b_eyeb"] = pr["eyeb"]
    for k in ("whd", "bqk", "bv", "wout", "bout", "cosT", "sinT", "perm"):
        shared["c_" + k] = pr[k]
    shared["c_eyeb"] = pr["eyeb"]
    for k in ("W1b", "W2b", "H1b", "H2b", "W3b", "s1t", "s2t", "sh1t", "sh2t", "s3t"):
        shared["d_" + k] = pr[k]
    shared["d_eyeb"] = pr["eyeb"]
    maps = [dict(shared, x_in=x[c]) for c in cores]
    res = run_bass_kernel_spmd(nc, maps, core_ids=cores)
    return np.stack([res.results[c]["xout"] for c in cores]).astype(np.float32)
